# version 09
# baseline (speedup 1.0000x reference)
import math
import numpy as np
import ml_dtypes
import concourse.bass as bass
import concourse.mybir as mybir
from concourse.bass_utils import run_bass_kernel_spmd
from contextlib import ExitStack

F32 = mybir.dt.float32
BF16 = mybir.dt.bfloat16
F16 = mybir.dt.float16
I32 = mybir.dt.int32
AF = mybir.ActivationFunctionType
OP = mybir.AluOpType
AX = mybir.AxisListType.X

NCORE = 8
D = 2048
S = 8192
ROWS = 1024
NQT = 8
HD = 128
NH = 16
NKV = 8
IH = 5632
ISL = IH // NCORE
NDC = D // 128
NIC = IH // 128
NGR = 4
GCOL = ISL // NGR
EPS_LN = 1e-5
EPS_RMS = 1e-6
DSEC = 1024


def perm_block(c, i):
    return 8 * i + c


class Emit:
    __slots__ = ("role", "e", "sems", "rk", "rv")

    def __init__(self, role, eng, sems):
        self.role = role
        self.e = eng
        self.sems = sems

    def inc(self, name, instr, d=1):
        s = self.sems[name]
        s[1] += d
        if instr is not None:
            instr.then_inc(s[0], d)
        return s[1]

    def val(self, name):
        return self.sems[name][1]

    def wait(self, role, name, v=None):
        if self.role == role:
            self.e.wait_ge(self.sems[name][0], self.sems[name][1] if v is None else v)

    QMAP = {"qx": "sync", "qw": "sync", "qg": "gpsimd", "qf": "sync"}

    def dma(self, q, out, in_, sem):
        instr = self.e.dma_start(out=out, in_=in_) if self.role == self.QMAP[q] else None
        self.inc(sem, instr, 16)

    def do(self, role, fn):
        if self.role == role:
            return fn(self.e)
        return None

    def doi(self, role, sem, fn, d=1):
        instr = fn(self.e) if self.role == role else None
        self.inc(sem, instr, d)


SEM_NAMES = ["qx", "qw", "qg", "qf", "cc", "pe", "act", "dve", "gp", "ph",
             "ccr0", "ccr1", "ccr2", "ccr3", "ccr4", "ccr5", "ccd", "ccb", "cck", "ccv"]


def build(stage="x1", detect_races=False):
    nc = bass.Bass("TRN2", target_bir_lowering=False, debug=False, detect_race_conditions=detect_races)

    x_in = nc.dram_tensor("x_own", [ROWS, D], F32, kind="ExternalInput").ap()
    gu_packed = nc.dram_tensor("gu_packed", [1536, 256], I32, kind="ExternalInput").ap()
    gu_scales = nc.dram_tensor("gu_scales_t", [1536, 16], F16, kind="ExternalInput").ap()
    dn_packed = nc.dram_tensor("dn_packed", [256, IH // 8], I32, kind="ExternalInput").ap()
    dn_scales = nc.dram_tensor("dn_scales_t", [256, 44], F16, kind="ExternalInput").ap()
    ident_in = nc.dram_tensor("ident_f32", [128, 128], F32, kind="ExternalInput").ap()
    consts_in = nc.dram_tensor("consts", [128, 8], F32, kind="ExternalInput").ap()
    wq_in = nc.dram_tensor("wq", [D, D], F32, kind="ExternalInput").ap()
    wk_in = nc.dram_tensor("wk", [D, 1024], F32, kind="ExternalInput").ap()
    wv_in = nc.dram_tensor("wv", [D, 1024], F32, kind="ExternalInput").ap()
    wo_in = nc.dram_tensor("wo", [D, D], F32, kind="ExternalInput").ap()
    ropeC_in = nc.dram_tensor("ropeC", [128, ROWS], F32, kind="ExternalInput").ap()
    ropeS_in = nc.dram_tensor("ropeS", [128, ROWS], F32, kind="ExternalInput").ap()
    pswap_in = nc.dram_tensor("pswap", [128, 128], BF16, kind="ExternalInput").ap()
    causal_in = nc.dram_tensor("causal_neg", [128, 128], F32, kind="ExternalInput").ap()
    colmask_in = nc.dram_tensor("colmask", [128, 512], F32, kind="ExternalInput").ap()
    ownsel_in = nc.dram_tensor("ownsel", [128, 512], F32, kind="ExternalInput").ap()
    diag_in = nc.dram_tensor("diag_en", [128, 8], F32, kind="ExternalInput").ap()
    tA_in = nc.dram_tensor("tA", [128, 296], F32, kind="ExternalInput").ap()

    y_out = nc.dram_tensor("y_own", [ROWS, D], F32, kind="ExternalOutput").ap()
    xres_hbm = nc.dram_tensor("xres_hbm", [ROWS, D], F32)
    agw_in = [nc.dram_tensor(f"agw_in{r}", [256, D], BF16) for r in range(6)]
    agw_out = [nc.dram_tensor(f"agw_out{r}", [NCORE * 256, D], BF16) for r in range(6)]
    agd_in = nc.dram_tensor("agd_in", [IH, 256], BF16)
    agd_out = nc.dram_tensor("agd_out", [NCORE * IH, 256], BF16)
    kin = nc.dram_tensor("kin", [NKV * 128, ROWS], BF16)
    kout = nc.dram_tensor("kout", [NCORE * NKV * 128, ROWS], BF16)
    vin = nc.dram_tensor("vin", [ROWS, NKV * 128], BF16)
    vout = nc.dram_tensor("vout", [NCORE * ROWS, NKV * 128], BF16)
    bmin = nc.dram_tensor("bmin", [NKV * 128, 8], F32)
    bmout = nc.dram_tensor("bmout", [NCORE * NKV * 128, 8], F32)

    ctx = ExitStack()
    sb = lambda name, shape, dt: ctx.enter_context(nc.sbuf_tensor(name, shape, dt))

    xstage = sb("xstage", [128, 2, D], F32)
    x_lnT = sb("x_lnT", [128, NDC, 512], BF16)
    h_T = sb("h_T", [128, NIC, 512], BF16)
    q_sb = sb("q_sb", [128, NQT, NH, 128], BF16)
    ln_tok = sb("ln_tok", [128, D], F32)
    wstage2 = sb("wstage2", [128, D], F32)
    ln_s = sb("ln_s", [128, 4, 4], F32)
    ln_rstd = sb("ln_rstd", [128, 4], F32)
    ln_mb = sb("ln_mb", [128, 4], F32)
    ident = sb("identt", [128, 128], F32)
    consts_sb = sb("consts_sb", [128, 8], F32)
    nc.const_aps.register(F32, float(EPS_LN), consts_sb[:, 0:1])
    nc.const_aps.register(F32, float(EPS_RMS), consts_sb[:, 1:2])
    nc.const_aps.register(F32, 1.0 / D, consts_sb[:, 2:3])

    dq_pack = sb("dq_pack", [128, 2, 128], I32)
    dq_nib = sb("dq_nib", [128, DSEC], I32)
    dq_mi = sb("dq_mi", [128, DSEC], I32)
    dq_mf = sb("dq_mf", [128, DSEC], F32)
    dq_sf = sb("dq_sf", [128, DSEC], F32)
    dq_t1 = sb("dq_t1", [128, DSEC], F32)
    dq_v = sb("dq_v", [128, 2, DSEC], F32)
    dq_sc16 = sb("dq_sc16", [128, 2, 8], F16)
    dq_sc = sb("dq_sc", [128, 2, 8], F32)
    dq_out = sb("dq_out", [128, 8, 128], BF16)

    wg_buf = sb("wg_buf", [128, 2, 2, NDC, 128], BF16)
    wd_buf = sb("wd_buf", [128, 2, 4, 512], BF16)
    silu_t = sb("silu_t", [128, 4, 512], BF16)
    rms_s = sb("rms_s", [128, 16], F32)
    rms_c = sb("rms_c", [128, 8], F32)
    pswap_sb = sb("pswap_sb", [128, 128], BF16)
    causal_sb = sb("causal_sb", [128, 128], F32)
    colmask_sb = sb("colmask_sb", [128, 512], F32)
    ownsel_sb = sb("ownsel_sb", [128, 512], F32)
    diag_sb = sb("diag_sb", [128, 8], F32)
    tA_sb = sb("tA_sb", [128, 296], F32)

    bm_own = sb("bm_own", [128, NKV, 8], F32)
    bm_sb = sb("bm_sb", [128, NKV, 64], F32)
    bm_sbb = sb("bm_sbb", [128, NKV, 64], BF16)
    notF_sb = sb("notF_sb", [128, NH, 8], F32)
    recip_sb = sb("recip_sb", [128, NH + 2], F32)
    kbuf = sb("kbuf", [128, 2, NKV, 128], BF16)
    vbuf = sb("vbuf", [128, 2, NKV, 129], BF16)
    probs_sb = sb("probs_sb", [128, 2, 512], BF16)
    ropeC_sb = dq_t1
    ropeS_sb = None
    wv_alias = bass.AP(h_T, 0, [[h_T.shape[1] * h_T.shape[2], 128], [1024, 16], [1, 1024]])
    ropeS_sb = bass.AP(dq_v, 0, [[2 * DSEC, 128], [1, ROWS]])

    pb = [ctx.enter_context(nc.psum_tensor(f"pb{i}", [128, 512], F32)) for i in range(8)]

    sems = {}
    for name in SEM_NAMES:
        h = ctx.enter_context(nc.semaphore(name))
        sems[name] = [h, 0]

    blk = ctx.enter_context(nc.Block())

    def full_sync(em):
        for role in ("sync", "scalar", "vector", "tensor", "gpsimd"):
            for s in ("qx", "qw", "qg", "pe", "act", "dve"):
                em.wait(role, s)

    def ln_tile(em, src, qt, lq, gen, tap=False):
        slot = qt % 2
        em.wait("sync", "act")
        em.wait("sync", "qg")
        em.dma("qx", xstage[:, slot, :], src.rearrange("(q p) d -> q p d", p=128)[qt, :, :], "qx")
        em.wait("scalar", "qx")
        em.wait("scalar", "pe")
        em.doi("scalar", "act",
               lambda e: e.activation(out=ln_tok[:, :], in_=xstage[:, slot, :], func=AF.Square,
                                      accum_out=ln_s[:, 1, 0:1]))
        em.wait("vector", "act")
        em.doi("vector", "dve", lambda e: e.tensor_reduce(ln_s[:, 0, 0:1], xstage[:, slot, :], axis=AX, op=OP.add))
        em.wait("vector", "dve")
        em.doi("vector", "dve", lambda e: e.tensor_scalar(ln_s[:, 2, 0:1], ln_s[:, 0, 0:1], 1.0 / D, None, OP.mult))
        em.wait("vector", "dve")
        em.doi("vector", "dve", lambda e: e.tensor_tensor(ln_s[:, 3, 0:1], ln_s[:, 2, 0:1], ln_s[:, 2, 0:1], OP.mult))
        em.wait("vector", "dve")
        em.doi("vector", "dve", lambda e: e.scalar_tensor_tensor(ln_s[:, 3, 0:1], ln_s[:, 1, 0:1], 1.0 / D,
                                                                 ln_s[:, 3, 0:1], OP.mult, OP.subtract))
        em.wait("scalar", "dve")
        em.doi("scalar", "act", lambda e: e.activation(out=ln_rstd[:, 1:2], in_=ln_s[:, 3, 0:1], func=AF.Sqrt,
                                                       bias=float(EPS_LN)))
        em.wait("vector", "act")
        em.doi("vector", "dve", lambda e: e.reciprocal(ln_rstd[:, 0:1], ln_rstd[:, 1:2]))
        em.wait("vector", "dve")
        em.doi("vector", "dve", lambda e: e.tensor_tensor(ln_mb[:, 0:1], ln_s[:, 2, 0:1], ln_rstd[:, 0:1], OP.mult))
        em.wait("vector", "dve")
        em.doi("vector", "dve", lambda e: e.tensor_scalar(ln_mb[:, 0:1], ln_mb[:, 0:1], -1.0, None, OP.mult))
        em.wait("scalar", "dve")
        em.doi("scalar", "act",
               lambda e: e.activation(out=ln_tok[:, :], in_=xstage[:, slot, :], func=AF.Identity,
                                      scale=ln_rstd[:, 0:1], bias=ln_mb[:, 0:1]))
        em.wait("tensor", "act")
        for b in range(4):
            for k in range(4):
                dch = 4 * b + k
                em.doi("tensor", "pe",
                       lambda e, dch=dch, k=k: e.transpose(pb[4][:, k * 128:(k + 1) * 128],
                                                           ln_tok[:, dch * 128:(dch + 1) * 128], ident[:, :]))
            em.wait("vector", "pe")
            em.doi("vector", "dve",
                   lambda e, b=b, lq=lq: e.tensor_copy(
                       x_lnT[:, 4 * b:4 * b + 4, lq * 128:(lq + 1) * 128],
                       pb[4][:, :].rearrange("p (a b) -> p a b", a=4)))
            em.wait("tensor", "dve")

    def dequant_ag(em):
        gu3 = gu_packed.rearrange("(n p) w -> n p w", p=128)
        gs3 = gu_scales.rearrange("(n p) w -> n p w", p=128)
        dn3 = dn_packed.rearrange("(n p) w -> n p w", p=128)
        ds3 = dn_scales.rearrange("(n p) w -> n p w", p=128)

        dq_state = {"si": 0, "ext": [], "tr": []}

        def decode_sec(packed3, rt, w_off, nw, scales3, g_off, ngrp, store_fn):
            si = dq_state["si"]; dq_state["si"] = si + 1
            par = si % 2
            W = nw * 8
            if si >= 2:
                em.wait("gpsimd", "dve", dq_state["ext"][si - 2])
            em.dma("qg", dq_pack[:, par, 0:nw], packed3[rt, :, w_off:w_off + nw], "qg")
            em.dma("qg", dq_sc16[:, par, 0:ngrp], scales3[rt, :, g_off:g_off + ngrp], "qg")
            qgm = em.val("qg")
            em.wait("vector", "qg", qgm)
            em.doi("vector", "dve", lambda e: e.tensor_copy(dq_sc[:, par, 0:ngrp], dq_sc16[:, par, 0:ngrp]))
            for k in range(8):
                em.doi("vector", "dve",
                       lambda e, k=k: e.tensor_scalar(dq_nib[:, k:W:8], dq_pack[:, par, 0:nw], 4 * k, 15,
                                                      OP.logical_shift_right, OP.bitwise_and))
            dq_state["ext"].append(em.val("dve"))
            em.doi("vector", "dve", lambda e: e.tensor_scalar(dq_mi[:, 0:W], dq_nib[:, 0:W], 7, None, OP.bitwise_and))
            em.doi("vector", "dve", lambda e: e.tensor_copy(dq_mf[:, 0:W], dq_mi[:, 0:W]))
            em.doi("vector", "dve", lambda e: e.tensor_scalar(dq_mi[:, 0:W], dq_nib[:, 0:W], 3, None,
                                                              OP.logical_shift_right))
            em.doi("vector", "dve", lambda e: e.tensor_copy(dq_sf[:, 0:W], dq_mi[:, 0:W]))
            em.doi("vector", "dve", lambda e: e.tensor_scalar(dq_t1[:, 0:W], dq_mf[:, 0:W], -4.0, 0.0, OP.add, OP.max))
            if si >= 2:
                em.wait("vector", "pe", dq_state["tr"][si - 2])
            em.doi("vector", "dve", lambda e: e.tensor_tensor(dq_v[:, par, 0:W], dq_mf[:, 0:W], dq_t1[:, 0:W], OP.add))
            em.doi("vector", "dve", lambda e: e.tensor_scalar(dq_t1[:, 0:W], dq_mf[:, 0:W], -6.0, 0.0, OP.add, OP.max))
            em.doi("vector", "dve", lambda e: e.scalar_tensor_tensor(dq_v[:, par, 0:W], dq_v[:, par, 0:W], 0.5,
                                                                     dq_t1[:, 0:W], OP.mult, OP.add))
            em.doi("vector", "dve", lambda e: e.tensor_scalar(dq_sf[:, 0:W], dq_sf[:, 0:W], -2.0, 1.0, OP.mult, OP.add))
            em.doi("vector", "dve", lambda e: e.tensor_tensor(dq_v[:, par, 0:W], dq_v[:, par, 0:W], dq_sf[:, 0:W], OP.mult))
            for g in range(ngrp):
                em.doi("vector", "dve",
                       lambda e, g=g: e.tensor_scalar(dq_v[:, par, g * 128:(g + 1) * 128],
                                                      dq_v[:, par, g * 128:(g + 1) * 128],
                                                      dq_sc[:, par, g:g + 1], None, OP.mult))
            em.wait("tensor", "dve")
            em.wait("scalar", "qg")
            for b in range((ngrp + 3) // 4):
                kk = min(4, ngrp - 4 * b)
                for k in range(kk):
                    em.doi("tensor", "pe",
                           lambda e, b=b, k=k: e.transpose(pb[5][:, k * 128:(k + 1) * 128],
                                                           dq_v[:, par, (4 * b + k) * 128:(4 * b + k + 1) * 128],
                                                           ident[:, :]))
                em.wait("scalar", "pe")
                for k in range(kk):
                    em.doi("scalar", "act",
                           lambda e, b=b, k=k: e.copy(out=dq_out[:, 4 * b + k, :],
                                                      in_=pb[5][:, k * 128:(k + 1) * 128]))
                em.wait("tensor", "act")
            dq_state["tr"].append(em.val("pe"))
            em.wait("gpsimd", "act")
            for j in range(ngrp):
                store_fn(em, j)

        def mk_gu_store2(rt, s):
            k6, is_up = rt % 6, rt // 6
            def store(em, j):
                dch = 8 * s + j
                dst = agw_in[k6].ap()[is_up * 128:(is_up + 1) * 128, dch * 128:(dch + 1) * 128]
                em.dma("qg", dst, dq_out[:, j, :], "qg")
            return store

        for k6 in range(6):
            for rt in (k6, 6 + k6):
                for s in range(2):
                    decode_sec(gu3, rt, 128 * s, 128, gs3, 8 * s, 8, mk_gu_store2(rt, s))
            em.wait("gpsimd", "qg")
            em.doi("gpsimd", f"ccr{k6}",
                   lambda e, k6=k6: e.collective_compute("AllGather", OP.bypass,
                                                         replica_groups=[list(range(NCORE))],
                                                         ins=[agw_in[k6].ap().opt()],
                                                         outs=[agw_out[k6].ap().opt()]))

        def mk_dn_store(rt, s):
            def store(em, j):
                ich = 8 * s + j
                dst = agd_in.ap()[ich * 128:(ich + 1) * 128, rt * 128:(rt + 1) * 128]
                em.dma("qg", dst, dq_out[:, j, :], "qg")
            return store

        for rt in range(2):
            for s in range(6):
                ngrp = min(8, 44 - 8 * s)
                decode_sec(dn3, rt, 128 * s, min(128, 704 - 128 * s), ds3, 8 * s, ngrp, mk_dn_store(rt, s))

        em.wait("gpsimd", "qg")
        em.doi("gpsimd", "ccd",
               lambda e: e.collective_compute("AllGather", OP.bypass, replica_groups=[list(range(NCORE))],
                                              ins=[agd_in.ap().opt()], outs=[agd_out.ap().opt()]))
        for s in ("ccr0", "ccr1", "ccr2", "ccr3", "ccr4", "ccr5", "ccd"):
            em.wait("gpsimd", s, 1)
        em.doi("gpsimd", "ph", lambda e: e.engine_nop())
        for r in ("sync", "scalar", "vector", "tensor"):
            em.wait(r, "ph", em.val("ph"))

    def ffn(em, src, dst, half_scale, cc_base=None):
        chunks = [(k, c2) for k in range(6) for c2 in range(NCORE) if 8 * k + c2 < NIC]
        for half in range(2):
            for lq in range(4):
                ln_tile(em, src, half * 4 + lq, lq, 0, tap=(stage == "dbg" and half == 0 and lq == 0))
            em.wait("sync", "pe")
            em.wait("tensor", "dve")
            em.wait("scalar", "dve")
            pe_marks = []
            dve_marks1 = []
            for ci, (k, c2) in enumerate(chunks):
                g = 8 * k + c2
                buf = ci % 2
                if ci >= 2:
                    em.wait("sync", "pe", pe_marks[ci - 2])
                if cc_base is not None:
                    em.wait("sync", f"ccr{k}", 1)
                    if k < 5:
                        em.wait("sync", f"ccr{k + 1}", 1)
                    else:
                        em.wait("sync", "ccd", 1)
                base = agw_out[k].ap()
                em.dma("qw", wg_buf[:, buf, 0, :, :].rearrange("p a b -> p (a b)"),
                       base[c2 * 256:c2 * 256 + 128, :], "qw")
                em.dma("qw", wg_buf[:, buf, 1, :, :].rearrange("p a b -> p (a b)"),
                       base[c2 * 256 + 128:c2 * 256 + 256, :], "qw")
                em.wait("tensor", "qw")
                if ci >= 2:
                    em.wait("tensor", "dve", dve_marks1[ci - 2])
                pg, pu = pb[2 * buf], pb[2 * buf + 1]
                for dch in range(NDC):
                    em.doi("tensor", "pe",
                           lambda e, dch=dch, buf=buf, pg=pg: e.matmul(pg[:, :], wg_buf[:, buf, 0, dch, :],
                                                                       x_lnT[:, dch, :],
                                                                       start=(dch == 0), stop=(dch == NDC - 1)))
                for dch in range(NDC):
                    em.doi("tensor", "pe",
                           lambda e, dch=dch, buf=buf, pu=pu: e.matmul(pu[:, :], wg_buf[:, buf, 1, dch, :],
                                                                       x_lnT[:, dch, :],
                                                                       start=(dch == 0), stop=(dch == NDC - 1)))
                pe_marks.append(em.val("pe"))
                em.wait("scalar", "pe")
                if ci >= 2:
                    em.wait("scalar", "dve", dve_marks1[ci - 2])
                em.doi("scalar", "act", lambda e, pg=pg, ci=ci: e.activation(out=silu_t[:, ci % 2, :], in_=pg[:, :], func=AF.Silu))
                em.wait("vector", "act")
                em.doi("vector", "dve",
                       lambda e, g=g, ci=ci, pu=pu: e.tensor_tensor(h_T[:, g, :], silu_t[:, ci % 2, :], pu[:, :], OP.mult))
                dve_marks1.append(em.val("dve"))

            em.wait("tensor", "dve")
            for pair in range(2):
                qts = (half * 4 + pair * 2, half * 4 + pair * 2 + 1)
                dve_marks = []
                em.wait("sync", "pe")
                pe_marks2 = []
                for ic in range(NIC):
                    buf = ic % 2
                    if ic >= 2:
                        em.wait("sync", "pe", pe_marks2[ic - 2])
                    em.dma("qw", wd_buf[:, buf, :, :].rearrange("p a b -> p (a b)"),
                           bass.AP(agd_out, ic * 128 * 256,
                                   [[256, 128], [IH * 256, 8], [1, 256]]), "qw")
                    em.wait("tensor", "qw")
                    for qi in range(2):
                        lq = qts[qi] - half * 4
                        for dg in range(4):
                            em.doi("tensor", "pe",
                                   lambda e, ic=ic, buf=buf, qi=qi, lq=lq, dg=dg: e.matmul(
                                       pb[qi * 4 + dg][:, :], h_T[:, ic, lq * 128:(lq + 1) * 128],
                                       wd_buf[:, buf, dg, :],
                                       start=(ic == 0), stop=(ic == NIC - 1)))
                    pe_marks2.append(em.val("pe"))
                em.wait("scalar", "pe")
                for qi in range(2):
                    for dg in range(4):
                        em.doi("scalar", "act",
                               lambda e, qi=qi, dg=dg: e.activation(out=ln_tok[:, 0:512], in_=pb[qi * 4 + dg][:, :],
                                                                    func=AF.Square,
                                                                    accum_out=rms_s[:, qi * 8 + dg:qi * 8 + dg + 1]))
                for qi in range(2):
                    em.doi("scalar", "act",
                           lambda e, qi=qi: e.copy(out=rms_c[:, qi * 4:qi * 4 + 4], in_=rms_s[:, qi * 8:qi * 8 + 4]))
                em.wait("vector", "act")
                for qi in range(2):
                    em.doi("vector", "dve",
                           lambda e, qi=qi: e.tensor_reduce(rms_s[:, qi * 8 + 4:qi * 8 + 5],
                                                            rms_c[:, qi * 4:qi * 4 + 4], axis=AX, op=OP.add))
                em.wait("vector", "dve")
                em.wait("scalar", "dve")
                for qi in range(2):
                    em.doi("scalar", "act",
                           lambda e, qi=qi: e.activation(out=rms_s[:, qi * 8 + 5:qi * 8 + 6],
                                                         in_=rms_s[:, qi * 8 + 4:qi * 8 + 5],
                                                         func=AF.Sqrt, bias=float(EPS_RMS), scale=1.0 / D))
                em.wait("vector", "act")
                for qi in range(2):
                    em.doi("vector", "dve",
                           lambda e, qi=qi: e.reciprocal(rms_s[:, qi * 8 + 7:qi * 8 + 8],
                                                         rms_s[:, qi * 8 + 5:qi * 8 + 6]))
                    em.wait("vector", "dve")
                    em.doi("vector", "dve",
                           lambda e, qi=qi: e.tensor_scalar(rms_s[:, qi * 8 + 6:qi * 8 + 7],
                                                            rms_s[:, qi * 8 + 7:qi * 8 + 8],
                                                            float(half_scale), None, OP.mult))
                em.wait("sync", "act")
                em.wait("sync", "qg")
                for qi in range(2):
                    qt = qts[qi]
                    em.dma("qx", xstage[:, qi, :], src.rearrange("(q p) d -> q p d", p=128)[qt, :, :], "qx")
                em.wait("vector", "qx")
                em.doi("vector", "ph", lambda e: e.engine_nop())
                em.wait("vector", "ph")
                for qi in range(2):
                    for dg in range(4):
                        em.doi("vector", "dve",
                               lambda e, qi=qi, dg=dg: e.scalar_tensor_tensor(
                                   xstage[:, qi, dg * 512:(dg + 1) * 512], pb[qi * 4 + dg][:, :],
                                   rms_s[:, qi * 8 + 6:qi * 8 + 7], xstage[:, qi, dg * 512:(dg + 1) * 512],
                                   OP.mult, OP.add))
                em.wait("gpsimd", "dve")
                for qi in range(2):
                    qt = qts[qi]
                    em.dma("qg", dst.rearrange("(q p) d -> q p d", p=128)[qt, :, :], xstage[:, qi, :], "qg")
                em.wait("tensor", "qg")
                em.wait("tensor", "dve")


    HT_ROW = NIC * 512
    QS_ROW = NQT * NH * 128

    def wv_ap(dch):
        return bass.AP(h_T, dch * 1024, [[HT_ROW, 128], [1, 1024]])

    def wo_ap(dch, lo, n):
        if dch < 11:
            return bass.AP(h_T, dch * 2048 + lo, [[HT_ROW, 128], [1, n]])
        if dch < 15:
            return bass.AP(wg_buf, (dch - 11) * 2048 + lo, [[8192, 128], [1, n]])
        return bass.AP(x_lnT, lo, [[8192, 128], [1, n]])

    def attnT_ap(dch):
        return bass.AP(x_lnT, 2048 + dch * 128, [[8192, 128], [1, 128]])

    def acc_ap(qh, lo, n):
        return pb[qh // 3][:, (qh % 3) * 129 + lo:(qh % 3) * 129 + lo + n]

    def load_consts_attn(em):
        em.dma("qx", pswap_sb[:, :], pswap_in, "qx")
        em.dma("qx", causal_sb[:, :], causal_in, "qx")
        em.dma("qx", colmask_sb[:, :], colmask_in, "qx")
        em.dma("qx", ownsel_sb[:, :], ownsel_in, "qx")
        em.dma("qx", diag_sb[:, :], diag_in, "qx")
        em.dma("qx", tA_sb[:, :], tA_in, "qx")

        em.wait("vector", "qx")
        for s in range(2):
            em.doi("vector", "dve", lambda e, s=s: e.memset(vbuf[:, s, :, 128:129], 1.0))

    def proj_attn(em, src):
        em.wait("sync", "dve")
        em.wait("sync", "act")
        em.dma("qx", ropeC_sb[:, :], ropeC_in, "qx")
        em.dma("qx", ropeS_sb[:, :], ropeS_in, "qx")
        for half in range(2):
            for lq in range(4):
                ln_tile(em, src, half * 4 + lq, lq, 1)
            if half == 0:
                for dch in range(NDC):
                    em.wait("sync", "dve")
                    em.dma("qw", dq_mf[:, :], wv_in.rearrange("(a p) n -> a p n", p=128)[dch, :, :], "qw")
                    em.wait("vector", "qw")
                    em.doi("vector", "dve", lambda e, dch=dch: e.tensor_copy(wv_ap(dch), dq_mf[:, :]))
                em.wait("tensor", "dve")
            for tt in range(4):
                em.wait("tensor", "act")
                for dch in range(NDC):
                    em.doi("tensor", "pe",
                           lambda e, dch=dch, tt=tt: e.matmul(pb[2][:, :], x_lnT[:, dch, tt * 128:(tt + 1) * 128],
                                                              wv_ap(dch)[:, 0:512],
                                                              start=(dch == 0), stop=(dch == NDC - 1)))
                for dch in range(NDC):
                    em.doi("tensor", "pe",
                           lambda e, dch=dch, tt=tt: e.matmul(pb[3][:, :], x_lnT[:, dch, tt * 128:(tt + 1) * 128],
                                                              wv_ap(dch)[:, 512:1024],
                                                              start=(dch == 0), stop=(dch == NDC - 1)))
                em.wait("scalar", "pe")
                em.wait("scalar", "qg")
                em.doi("scalar", "act", lambda e: e.copy(out=silu_t[:, 0, :], in_=pb[2][:, :]))
                em.doi("scalar", "act", lambda e: e.copy(out=silu_t[:, 1, :], in_=pb[3][:, :]))
                em.wait("gpsimd", "act")
                gt = (half * 4 + tt) * 128
                em.dma("qg", vin.ap()[gt:gt + 128, 0:512], silu_t[:, 0, :], "qg")
                em.dma("qg", vin.ap()[gt:gt + 128, 512:1024], silu_t[:, 1, :], "qg")
            heads = [("k", j) for j in range(NKV)] + [("q", j) for j in range(NH)]
            NHD = len(heads)
            mm_mark = [0] * NHD; conv_mark = [0] * NHD; evac_mark = [0] * NHD
            swap_mark = [0] * NHD; rope_mark = [0] * NHD; qwm = [0] * NHD

            def stage_ap(par):
                return (ln_tok if par == 0 else wstage2)[:, :]

            def wkb_ap(par, dch, n=128):
                return bass.AP(wg_buf, par * 4096 + dch * 128, [[8192, 128], [1, n]])

            def emit_swap(h2):
                kind2, j2 = heads[h2]; par2 = h2 % 2
                ps2 = pb[1] if par2 == 0 else pb[3]
                em.wait("tensor", "dve", conv_mark[h2])
                if h2 >= 2:
                    em.wait("tensor", "dve", rope_mark[h2 - 2])
                em.doi("tensor", "pe", lambda e, ps2=ps2, par2=par2: e.matmul(
                    ps2[:, :], pswap_sb[:, :], silu_t[:, par2, :], start=True, stop=True))
                swap_mark[h2] = em.val("pe")

            def emit_rope(h2):
                kind2, j2 = heads[h2]; par2 = h2 % 2
                ps2 = pb[1] if par2 == 0 else pb[3]
                em.wait("vector", "pe", swap_mark[h2])
                em.doi("vector", "dve",
                       lambda e, half=half, par2=par2: e.tensor_tensor(
                           dq_sf[:, 0:512], dq_mf[:, par2 * 512:par2 * 512 + 512],
                           ropeC_sb[:, half * 512:(half + 1) * 512], OP.mult))
                em.doi("vector", "dve",
                       lambda e, half=half, ps2=ps2: e.tensor_tensor(
                           dq_sf[:, 512:1024], ps2[:, :],
                           ropeS_sb[:, half * 512:(half + 1) * 512], OP.mult))
                em.wait("vector", "dve")
                if kind2 == "k":
                    em.wait("vector", "qg")
                    em.doi("vector", "dve",
                           lambda e: e.tensor_tensor(silu_t[:, 2, :], dq_sf[:, 0:512], dq_sf[:, 512:1024], OP.add))
                    em.wait("vector", "dve")
                    em.doi("vector", "dve",
                           lambda e, j2=j2, half=half: e.tensor_reduce(
                               bm_own[:, j2, half * 4:half * 4 + 4],
                               silu_t[:, 2, :].rearrange("p (a b) -> p a b", a=4), axis=AX, op=OP.add))
                    em.wait("gpsimd", "dve")
                    em.dma("qg", kin.ap()[j2 * 128:(j2 + 1) * 128, half * 512:(half + 1) * 512],
                           silu_t[:, 2, :], "qg")
                else:
                    em.doi("vector", "dve",
                           lambda e: e.tensor_tensor(dq_sf[:, 0:512], dq_sf[:, 0:512], dq_sf[:, 512:1024], OP.add))
                    em.wait("vector", "dve")
                    em.doi("vector", "dve",
                           lambda e, j2=j2, half=half: e.tensor_scalar(
                               bass.AP(q_sb, (half * 4) * 2048 + j2 * 128, [[QS_ROW, 128], [2048, 4], [1, 128]]),
                               dq_sf[:, 0:512].rearrange("p (a b) -> p a b", a=4),
                               float(1.0 / math.sqrt(HD)), None, OP.mult))
                rope_mark[h2] = em.val("dve")

            for hi, (kind, j) in enumerate(heads):
                par = hi % 2
                w_in = wk_in if kind == "k" else wq_in
                pq = pb[0] if par == 0 else pb[2]
                if hi >= 2:
                    em.wait("sync", "pe", mm_mark[hi - 2])
                    em.wait("sync", "dve", conv_mark[hi - 2])
                elif hi == 0:
                    em.wait("sync", "pe")
                for dch in range(NDC):
                    em.dma("qw", stage_ap(par)[:, dch * 128:(dch + 1) * 128],
                           w_in[dch * 128:(dch + 1) * 128, j * 128:(j + 1) * 128], "qw")
                qwm[hi] = em.val("qw")
                em.wait("vector", "qw", qwm[hi])
                if hi >= 2:
                    em.wait("vector", "pe", mm_mark[hi - 2])
                em.doi("vector", "dve", lambda e, par=par: e.tensor_copy(wkb_ap(par, 0, 2048), stage_ap(par)))
                conv_mark[hi] = em.val("dve")
                em.wait("tensor", "dve", conv_mark[hi])
                if hi >= 2:
                    em.wait("tensor", "act", evac_mark[hi - 2])
                for dch in range(NDC):
                    em.doi("tensor", "pe",
                           lambda e, dch=dch, par=par, pq=pq: e.matmul(pq[:, :], wkb_ap(par, dch),
                                                                      x_lnT[:, dch, :],
                                                                      start=(dch == 0), stop=(dch == NDC - 1)))
                mm_mark[hi] = em.val("pe")
                em.wait("scalar", "pe", mm_mark[hi])
                if hi >= 2:
                    em.wait("scalar", "dve", rope_mark[hi - 2])
                else:
                    em.wait("scalar", "dve")
                em.doi("scalar", "act", lambda e, pq=pq, par=par: e.copy(out=dq_mf[:, par * 512:par * 512 + 512],
                                                                        in_=pq[:, :]))
                evac_mark[hi] = em.val("act")
                em.wait("vector", "act", evac_mark[hi])
                if hi >= 2:
                    em.wait("vector", "pe", swap_mark[hi - 2])
                else:
                    em.wait("vector", "qg")
                em.doi("vector", "dve", lambda e, par=par: e.tensor_copy(silu_t[:, par, :],
                                                                         dq_mf[:, par * 512:par * 512 + 512]))
                conv_mark[hi] = em.val("dve")
                if hi >= 1:
                    emit_swap(hi - 1)
                    emit_rope(hi - 1)
            emit_swap(NHD - 1)
            emit_rope(NHD - 1)
        em.wait("gpsimd", "dve")
        for kvh in range(NKV):
            em.dma("qg", bmin.ap()[kvh * 128:(kvh + 1) * 128, :],
                   bm_own[:, kvh, :], "qg")

    def ag_kv(em):
        em.wait("gpsimd", "qg")
        em.doi("gpsimd", "ccb",
               lambda e: e.collective_compute("AllGather", OP.bypass, replica_groups=[list(range(NCORE))],
                                              ins=[bmin.ap().opt()], outs=[bmout.ap().opt()]))
        em.doi("gpsimd", "cck",
               lambda e: e.collective_compute("AllGather", OP.bypass, replica_groups=[list(range(NCORE))],
                                              ins=[kin.ap().opt()], outs=[kout.ap().opt()]))
        em.doi("gpsimd", "ccv",
               lambda e: e.collective_compute("AllGather", OP.bypass, replica_groups=[list(range(NCORE))],
                                              ins=[vin.ap().opt()], outs=[vout.ap().opt()]))
        return 0

    def wo_load(em):
        for dch in range(NDC):
            em.wait("sync", "dve")
            em.dma("qw", ln_tok[:, :], wo_in.rearrange("(a p) n -> a p n", p=128)[dch, :, :], "qw")
            em.wait("vector", "qw")
            em.doi("vector", "dve", lambda e, dch=dch: e.tensor_copy(wo_ap(dch, 0, 2048), ln_tok[:, :]))

    def gating(em, cc_bm):
        em.wait("sync", "ccb", 1)
        for kvh in range(NKV):
            em.dma("qx", bm_sb[:, kvh, :],
                   bass.AP(bmout, kvh * 128 * 8, [[8, 128], [8192, 8], [1, 8]]), "qx")
        em.wait("vector", "qx")
        em.doi("vector", "dve", lambda e: e.tensor_copy(bm_sbb[:, :, :], bm_sb[:, :, :]))
        em.wait("tensor", "dve")
        for qh in range(NH):
            em.wait("tensor", "dve")
            for i in range(NQT):
                em.doi("tensor", "pe",
                       lambda e, i=i, qh=qh: e.matmul(pb[6][:, i * 64:(i + 1) * 64],
                                                      bass.AP(q_sb, i * 2048 + qh * 128, [[QS_ROW, 128], [1, 128]]),
                                                      bm_sbb[:, qh // 2, :], start=True, stop=True))
            em.wait("vector", "pe")
            em.doi("vector", "dve", lambda e: e.tensor_copy(ln_tok[:, 0:512], pb[6][:, :]))
            em.wait("vector", "dve")
            em.doi("vector", "dve", lambda e: e.tensor_tensor(ln_tok[:, 512:1024], ln_tok[:, 0:512],
                                                              ownsel_sb[:, :], OP.mult))
            em.wait("vector", "dve")
            em.doi("vector", "dve", lambda e: e.tensor_reduce(rms_s[:, 0:8],
                                                              ln_tok[:, 512:1024].rearrange("p (a b) -> p a b", a=8),
                                                              axis=AX, op=OP.add))
            em.wait("vector", "dve")
            em.doi("vector", "dve", lambda e: e.tensor_tensor(ln_tok[:, 512:1024], ln_tok[:, 0:512],
                                                              bass.AP(rms_s, 0, [[16, 128], [1, 8], [0, 64]]),
                                                              OP.is_gt))
            em.wait("vector", "dve")
            em.doi("vector", "dve", lambda e: e.tensor_tensor(ln_tok[:, 512:1024], ln_tok[:, 512:1024],
                                                              colmask_sb[:, :], OP.mult))
            em.wait("vector", "dve")
            em.doi("vector", "dve", lambda e: e.tensor_reduce(rms_s[:, 8:16],
                                                              ln_tok[:, 512:1024].rearrange("p (a b) -> p a b", a=8),
                                                              axis=AX, op=OP.add))
            em.wait("vector", "dve")
            em.doi("vector", "dve", lambda e: e.tensor_scalar(rms_s[:, 0:8], rms_s[:, 8:16], 7.5, None, OP.is_gt))
            em.wait("vector", "dve")
            em.doi("vector", "dve", lambda e, qh=qh: e.tensor_tensor(notF_sb[:, qh, :], rms_s[:, 0:8],
                                                                     diag_sb[:, :], OP.mult))
        em.wait("vector", "dve") if False else None

    def flash(em, dst):
        src_attn = xres_hbm.ap()
        em.wait("sync", "cck", 1)
        em.wait("sync", "ccv", 1)
        step_idx = 0
        pe_marks_f = []
        bank_act = [0, 0]
        bank_dve = [0, 0]
        probs_pe = [0, 0]
        for i in range(NQT):
            nsteps = 8 * i + 9
            em.wait("tensor", "dve")
            for st in range(nsteps):
                buf = step_idx % 2
                if step_idx >= 2:
                    em.wait("sync", "pe", pe_marks_f[step_idx - 2])
                if st == 0:
                    koff, kt = i * 128, kin
                    voff, vt = (i * 128) * 1024, vin
                else:
                    g = st - 1
                    c2, slot = g % 8, g // 8
                    koff, kt = c2 * 1024 * 1024 + slot * 128, kout
                    voff, vt = (c2 * 1024 + slot * 128) * 1024, vout
                em.dma("qf", kbuf[:, buf, :, :],
                       bass.AP(kt, koff, [[1024, 128], [131072, 8], [1, 128]]), "qf")
                em.dma("qf", vbuf[:, buf, :, 0:128],
                       bass.AP(vt, voff, [[1024, 128], [128, 8], [1, 128]]), "qf")
                em.wait("tensor", "qf")
                if st == 0:
                    for p in range(4):
                        em.wait("tensor", "act", bank_act[0])
                        for j in range(4):
                            qh = 4 * p + j
                            em.doi("tensor", "pe",
                                   lambda e, i=i, qh=qh, j=j, buf=buf: e.matmul(
                                       pb[6][:, j * 128:(j + 1) * 128],
                                       bass.AP(q_sb, i * 2048 + qh * 128, [[QS_ROW, 128], [1, 128]]),
                                       kbuf[:, buf, qh // 2, :], start=True, stop=True))
                        em.wait("vector", "pe")
                        for j in range(4):
                            qh = 4 * p + j
                            em.doi("vector", "dve",
                                   lambda e, qh=qh, j=j, i=i: e.scalar_tensor_tensor(
                                       pb[6][:, j * 128:(j + 1) * 128], causal_sb[:, :],
                                       notF_sb[:, qh, i:i + 1], pb[6][:, j * 128:(j + 1) * 128],
                                       OP.mult, OP.add))
                        em.wait("scalar", "dve")
                        em.doi("scalar", "act",
                               lambda e, s=step_idx: e.activation(out=dq_mf[:, 0:512], in_=pb[6][:, :],
                                                                  func=AF.Exp, bias=tA_sb[:, s:s + 1]))
                        bank_act[0] = em.val("act")
                        em.wait("tensor", "act")
                        em.wait("tensor", "act", bank_act[1])
                        for j in range(4):
                            em.doi("tensor", "pe",
                                   lambda e, j=j: e.transpose(pb[7][:, j * 128:(j + 1) * 128],
                                                              dq_mf[:, j * 128:(j + 1) * 128], ident[:, :]))
                        em.wait("vector", "pe")
                        em.wait("vector", "pe", probs_pe[p % 2])
                        em.doi("vector", "dve",
                               lambda e, p=p: e.tensor_copy(probs_sb[:, p % 2, :], pb[7][:, :]))
                        bank_dve[1] = em.val("dve")
                        em.wait("tensor", "dve")
                        for j in range(4):
                            qh = 4 * p + j
                            em.doi("tensor", "pe",
                                   lambda e, p=p, j=j, qh=qh, buf=buf: e.matmul(
                                       acc_ap(qh, 0, 129),
                                       probs_sb[:, p % 2, j * 128:(j + 1) * 128],
                                       vbuf[:, buf, qh // 2, :],
                                       start=(qh % 3 == 0), stop=False))
                        probs_pe[p % 2] = em.val("pe")
                else:
                    exp_marks = [None] * 4
                    def emit_pv(p, buf=buf, st=st, nsteps=nsteps):
                        b = p % 2
                        em.wait("tensor", "act", exp_marks[p])
                        for j in range(2):
                            for qq in range(2):
                                qh = 2 * (2 * p + j) + qq
                                em.doi("tensor", "pe",
                                       lambda e, b=b, j=j, qq=qq, qh=qh, buf=buf, st=st, nsteps=nsteps: e.matmul(
                                           acc_ap(qh, 0, 129),
                                           probs_sb[:, b, (j * 2 + qq) * 128:(j * 2 + qq + 1) * 128],
                                           vbuf[:, buf, qh // 2, :], start=False,
                                           stop=(st == nsteps - 1 and (qh % 3 == 2 or qh == 15))))
                        probs_pe[b] = em.val("pe")
                    for p in range(4):
                        b = p % 2
                        em.wait("tensor", "act", bank_act[b])
                        em.wait("tensor", "dve", bank_dve[b])
                        for j in range(2):
                            kvh = 2 * p + j
                            em.doi("tensor", "pe",
                                   lambda e, i=i, kvh=kvh, j=j, buf=buf, b=b: e.matmul(
                                       pb[6 + b][:, j * 256:(j + 1) * 256],
                                       kbuf[:, buf, kvh, :],
                                       bass.AP(q_sb, i * 2048 + 2 * kvh * 128, [[QS_ROW, 128], [1, 256]]),
                                       start=True, stop=True))
                        scores_mark = em.val("pe")
                        em.wait("scalar", "pe", scores_mark)
                        em.wait("scalar", "pe", probs_pe[b])
                        em.doi("scalar", "act",
                               lambda e, b=b, s=step_idx: e.activation(out=probs_sb[:, b, :], in_=pb[6 + b][:, :],
                                                                       func=AF.Exp, bias=tA_sb[:, s:s + 1]))
                        bank_act[b] = em.val("act")
                        exp_marks[p] = em.val("act")
                        if p >= 1:
                            emit_pv(p - 1)
                    emit_pv(3)
                pe_marks_f.append(em.val("pe"))
                step_idx += 1
            em.wait("vector", "pe")
            for qh in range(NH):
                em.doi("vector", "dve",
                       lambda e, qh=qh: e.reciprocal(recip_sb[:, qh:qh + 1], acc_ap(qh, 128, 1)))
            em.wait("vector", "dve")
            for qh in range(NH):
                em.doi("vector", "dve",
                       lambda e, qh=qh: e.tensor_scalar(ln_tok[:, qh * 128:(qh + 1) * 128],
                                                        acc_ap(qh, 0, 128),
                                                        recip_sb[:, qh:qh + 1], None, OP.mult))
            em.wait("tensor", "dve")
            em.wait("tensor", "act")
            for bq in range(4):
                for k in range(4):
                    dch = 4 * bq + k
                    em.doi("tensor", "pe",
                           lambda e, dch=dch, k=k: e.transpose(pb[6][:, k * 128:(k + 1) * 128],
                                                               ln_tok[:, dch * 128:(dch + 1) * 128], ident[:, :]))
                em.wait("vector", "pe")
                em.doi("vector", "dve",
                       lambda e, bq=bq: e.tensor_copy(
                           bass.AP(x_lnT, 2048 + (4 * bq) * 128, [[8192, 128], [1, 512]]),
                           pb[6][:, :]))
                em.wait("tensor", "dve")
            bank_dve[0] = em.val("dve")
            for dch in range(NDC):
                for dg in range(4):
                    em.doi("tensor", "pe",
                           lambda e, dch=dch, dg=dg: e.matmul(pb[dg][:, :], attnT_ap(dch),
                                                              wo_ap(dch, dg * 512, 512),
                                                              start=(dch == 0), stop=(dch == NDC - 1)))
            em.wait("sync", "act")
            em.wait("sync", "qg")
            em.dma("qx", xstage[:, 0, :], src_attn.rearrange("(q p) d -> q p d", p=128)[i, :, :], "qx")
            em.wait("vector", "qx")
            em.wait("vector", "pe")
            for dg in range(4):
                em.doi("vector", "dve",
                       lambda e, dg=dg: e.tensor_tensor(xstage[:, 0, dg * 512:(dg + 1) * 512],
                                                        xstage[:, 0, dg * 512:(dg + 1) * 512],
                                                        pb[dg][:, :], OP.add))
            em.wait("gpsimd", "dve")
            em.dma("qg", dst.rearrange("(q p) d -> q p d", p=128)[i, :, :], xstage[:, 0, :], "qg")

    def program(em):
        em.dma("qx", ident[:, :], ident_in, "qx")
        em.dma("qx", consts_sb[:, :], consts_in, "qx")
        if stage != "x1":
            load_consts_attn(em)
        cc_base0 = em.val("cc")
        dequant_ag(em)
        ffn(em, x_in, y_out if stage == "x1" else xres_hbm, 0.5, cc_base=cc_base0)
        if stage in ("x2", "full"):
            proj_attn(em, xres_hbm.ap())
            cc_bm = ag_kv(em)
            wo_load(em)
            gating(em, cc_bm)
            flash(em, y_out if stage == "x2" else xres_hbm)
        if stage == "full":
            ffn(em, xres_hbm, y_out, 0.5)
        em.wait("gpsimd", "qg")

    def run_pass(role, eng):
        for s in sems.values():
            s[1] = 0
        em = Emit(role, eng, sems)
        program(em)

    @blk.sync
    def _(e):
        run_pass("sync", e)

    @blk.scalar
    def _(e):
        run_pass("scalar", e)

    @blk.vector
    def _(e):
        run_pass("vector", e)

    @blk.tensor
    def _(e):
        run_pass("tensor", e)

    @blk.gpsimd
    def _(e):
        run_pass("gpsimd", e)

    ctx.close()
    return nc


def make_inputs(full, c):
    rows = []
    for i in range(8):
        g = perm_block(c, i)
        rows.extend(range(g * 128, g * 128 + 128))
    rows = np.array(rows)
    gup = full["L0_ffn_gate_up_packed"]
    gus = full["L0_ffn_gate_up_scales"]
    dnp = full["L0_ffn_down_packed"]
    dns = full["L0_ffn_down_scales"]
    gp_pad = np.zeros((1536, 256), np.int32)
    gs_pad = np.zeros((1536, 16), np.float16)
    for k in range(6):
        g = 8 * k + c
        if g < NIC:
            gp_pad[k * 128:(k + 1) * 128] = gup[g * 128:(g + 1) * 128]
            gs_pad[k * 128:(k + 1) * 128] = gus[:, g * 128:(g + 1) * 128].T
            gp_pad[(6 + k) * 128:(7 + k) * 128] = gup[IH + g * 128:IH + (g + 1) * 128]
            gs_pad[(6 + k) * 128:(7 + k) * 128] = gus[:, IH + g * 128:IH + (g + 1) * 128].T
    d = {
        "x_own": np.ascontiguousarray(full["L0_attn_x"][0][rows]).astype(np.float32),
        "gu_packed": gp_pad,
        "gu_scales_t": gs_pad,
        "dn_packed": np.ascontiguousarray(dnp[256 * c:256 * c + 256]),
        "dn_scales_t": np.ascontiguousarray(dns[:, 256 * c:256 * c + 256].T),
        "ident_f32": np.eye(128, dtype=np.float32),
        "consts": np.tile(np.array([[EPS_LN, EPS_RMS, 1.0 / D, 0, 0, 0, 0, 0]], np.float32), (128, 1)),
        "wq": full["L0_attn_wq"].astype(np.float32),
        "wk": full["L0_attn_wk"].astype(np.float32),
        "wv": full["L0_attn_wv"].astype(np.float32),
        "wo": full["L0_attn_wo"].astype(np.float32),
    }
    pos = rows.astype(np.float64)
    invf = 1.0 / (10000.0 ** (np.arange(0, HD, 2, dtype=np.float64) / HD))
    ang = pos[None, :] * np.repeat(invf, 2)[:, None]
    C = np.cos(ang).astype(np.float32)
    Ssin = np.sin(ang).astype(np.float32)
    sgn = np.where(np.arange(HD) % 2 == 0, -1.0, 1.0).astype(np.float32)
    d["ropeC"] = np.ascontiguousarray(C)
    d["ropeS"] = np.ascontiguousarray(Ssin * sgn[:, None])
    P = np.zeros((128, 128), np.float32)
    for cc in range(128):
        P[cc, cc ^ 1] = 1.0
    d["pswap"] = P.astype(ml_dtypes.bfloat16)
    cn = np.where(np.arange(128)[None, :] > np.arange(128)[:, None], -10000.0, 0.0).astype(np.float32)
    d["causal_neg"] = np.ascontiguousarray(cn)
    colg = np.array([8 * (col % 8) + col // 8 for col in range(64)])
    colmask = np.zeros((8, 64), np.float32)
    ownsel = np.zeros((8, 64), np.float32)
    for i in range(8):
        own = perm_block(c, i)
        colmask[i] = (colg <= own).astype(np.float32)
        ownsel[i, 8 * c + i] = 1.0
    d["colmask"] = np.tile(colmask.reshape(1, 512), (128, 1))
    d["ownsel"] = np.tile(ownsel.reshape(1, 512), (128, 1))
    d["diag_en"] = np.tile(np.array([[0.0] + [1.0] * 7], np.float32), (128, 1))
    tA = np.zeros(296, np.float32)
    s = 0
    for i in range(8):
        own = perm_block(c, i)
        tA[s] = 0.0; s += 1
        for g in range(8 * i + 8):
            if i == 0:
                tA[s] = -10000.0 if g == own else 0.0
            else:
                tA[s] = 0.0 if g < own else -10000.0
            s += 1
    assert s == 296
    d["tA"] = np.tile(tA[None, :], (128, 1))
    return d, rows




_CACHED = {}


def kernel(L0_attn_x, L0_attn_wq, L0_attn_wk, L0_attn_wv, L0_attn_wo,
           L0_ffn_gate_up_packed, L0_ffn_gate_up_scales,
           L0_ffn_down_packed, L0_ffn_down_scales):
    full = {
        "L0_attn_x": np.asarray(L0_attn_x),
        "L0_attn_wq": np.asarray(L0_attn_wq),
        "L0_attn_wk": np.asarray(L0_attn_wk),
        "L0_attn_wv": np.asarray(L0_attn_wv),
        "L0_attn_wo": np.asarray(L0_attn_wo),
        "L0_ffn_gate_up_packed": np.asarray(L0_ffn_gate_up_packed),
        "L0_ffn_gate_up_scales": np.asarray(L0_ffn_gate_up_scales),
        "L0_ffn_down_packed": np.asarray(L0_ffn_down_packed),
        "L0_ffn_down_scales": np.asarray(L0_ffn_down_scales),
    }
    if "nc" not in _CACHED:
        _CACHED["nc"] = build("full")
    nc = _CACHED["nc"]
    in_maps, rowl = [], []
    for c in range(NCORE):
        d, rows = make_inputs(full, c)
        in_maps.append(d)
        rowl.append(rows)
    res = run_bass_kernel_spmd(nc, in_maps, core_ids=list(range(NCORE)))
    out = np.zeros((1, S, D), np.float32)
    for c in range(NCORE):
        out[0, rowl[c]] = res.results[c]["y_own"]
    return out


# version 10
# speedup vs baseline: 1.0414x; 1.0414x over previous
import math
import numpy as np
import ml_dtypes
import concourse.bass as bass
import concourse.mybir as mybir
from concourse.bass_utils import run_bass_kernel_spmd
from contextlib import ExitStack

F32 = mybir.dt.float32
BF16 = mybir.dt.bfloat16
F16 = mybir.dt.float16
I32 = mybir.dt.int32
AF = mybir.ActivationFunctionType
OP = mybir.AluOpType
AX = mybir.AxisListType.X

NCORE = 8
D = 2048
S = 8192
ROWS = 1024
NQT = 8
HD = 128
NH = 16
NKV = 8
IH = 5632
ISL = IH // NCORE
NDC = D // 128
NIC = IH // 128
NGR = 4
GCOL = ISL // NGR
EPS_LN = 1e-5
EPS_RMS = 1e-6
DSEC = 1024


def perm_block(c, i):
    return 8 * i + c


class Emit:
    __slots__ = ("role", "e", "sems", "rk", "rv")

    def __init__(self, role, eng, sems):
        self.role = role
        self.e = eng
        self.sems = sems

    def inc(self, name, instr, d=1):
        s = self.sems[name]
        s[1] += d
        if instr is not None:
            instr.then_inc(s[0], d)
        return s[1]

    def val(self, name):
        return self.sems[name][1]

    def wait(self, role, name, v=None):
        if self.role == role:
            self.e.wait_ge(self.sems[name][0], self.sems[name][1] if v is None else v)

    QMAP = {"qx": "sync", "qw": "sync", "qg": "gpsimd", "qf": "sync"}

    def dma(self, q, out, in_, sem):
        instr = self.e.dma_start(out=out, in_=in_) if self.role == self.QMAP[q] else None
        self.inc(sem, instr, 16)

    def do(self, role, fn):
        if self.role == role:
            return fn(self.e)
        return None

    def doi(self, role, sem, fn, d=1):
        instr = fn(self.e) if self.role == role else None
        self.inc(sem, instr, d)


SEM_NAMES = ["qx", "qw", "qg", "qf", "cc", "pe", "act", "dve", "gp", "ph",
             "ccr0", "ccr1", "ccr2", "ccr3", "ccr4", "ccr5", "ccd", "ccb", "cck", "ccv"]


def build(stage="x1", detect_races=False):
    nc = bass.Bass("TRN2", target_bir_lowering=False, debug=False, detect_race_conditions=detect_races)

    x_in = nc.dram_tensor("x_own", [ROWS, D], F32, kind="ExternalInput").ap()
    gu_packed = nc.dram_tensor("gu_packed", [1536, 256], I32, kind="ExternalInput").ap()
    gu_scales = nc.dram_tensor("gu_scales_t", [1536, 16], F16, kind="ExternalInput").ap()
    dn_packed = nc.dram_tensor("dn_packed", [256, IH // 8], I32, kind="ExternalInput").ap()
    dn_scales = nc.dram_tensor("dn_scales_t", [256, 44], F16, kind="ExternalInput").ap()
    ident_in = nc.dram_tensor("ident_f32", [128, 128], F32, kind="ExternalInput").ap()
    consts_in = nc.dram_tensor("consts", [128, 8], F32, kind="ExternalInput").ap()
    wq_in = nc.dram_tensor("wq", [D, D], F32, kind="ExternalInput").ap()
    wk_in = nc.dram_tensor("wk", [D, 1024], F32, kind="ExternalInput").ap()
    wv_in = nc.dram_tensor("wv", [D, 1024], F32, kind="ExternalInput").ap()
    wo_in = nc.dram_tensor("wo", [D, D], F32, kind="ExternalInput").ap()
    ropeC_in = nc.dram_tensor("ropeC", [128, ROWS], F32, kind="ExternalInput").ap()
    ropeS_in = nc.dram_tensor("ropeS", [128, ROWS], F32, kind="ExternalInput").ap()
    pswap_in = nc.dram_tensor("pswap", [128, 128], BF16, kind="ExternalInput").ap()
    causal_in = nc.dram_tensor("causal_neg", [128, 128], F32, kind="ExternalInput").ap()
    colmask_in = nc.dram_tensor("colmask", [128, 512], F32, kind="ExternalInput").ap()
    ownsel_in = nc.dram_tensor("ownsel", [128, 512], F32, kind="ExternalInput").ap()
    diag_in = nc.dram_tensor("diag_en", [128, 8], F32, kind="ExternalInput").ap()
    tA_in = nc.dram_tensor("tA", [128, 296], F32, kind="ExternalInput").ap()

    y_out = nc.dram_tensor("y_own", [ROWS, D], F32, kind="ExternalOutput").ap()
    xres_hbm = nc.dram_tensor("xres_hbm", [ROWS, D], F32)
    agw_in = [nc.dram_tensor(f"agw_in{r}", [256, D], BF16) for r in range(6)]
    agw_out = [nc.dram_tensor(f"agw_out{r}", [NCORE * 256, D], BF16) for r in range(6)]
    agd_in = nc.dram_tensor("agd_in", [IH, 256], BF16)
    agd_out = nc.dram_tensor("agd_out", [NCORE * IH, 256], BF16)
    kin = nc.dram_tensor("kin", [NKV * 128, ROWS], BF16)
    kout = nc.dram_tensor("kout", [NCORE * NKV * 128, ROWS], BF16)
    vin = nc.dram_tensor("vin", [ROWS, NKV * 128], BF16)
    vout = nc.dram_tensor("vout", [NCORE * ROWS, NKV * 128], BF16)
    bmin = nc.dram_tensor("bmin", [NKV * 128, 8], F32)
    bmout = nc.dram_tensor("bmout", [NCORE * NKV * 128, 8], F32)

    ctx = ExitStack()
    sb = lambda name, shape, dt: ctx.enter_context(nc.sbuf_tensor(name, shape, dt))

    xstage = sb("xstage", [128, 2, D], F32)
    x_lnT = sb("x_lnT", [128, NDC, 512], BF16)
    h_T = sb("h_T", [128, NIC, 512], BF16)
    q_sb = sb("q_sb", [128, NQT, NH, 128], BF16)
    ln_tok = sb("ln_tok", [128, D], F32)
    wstage2 = sb("wstage2", [128, D], F32)
    ln_s = sb("ln_s", [128, 4, 4], F32)
    ln_rstd = sb("ln_rstd", [128, 4], F32)
    ln_mb = sb("ln_mb", [128, 4], F32)
    ident = sb("identt", [128, 128], F32)
    consts_sb = sb("consts_sb", [128, 8], F32)
    nc.const_aps.register(F32, float(EPS_LN), consts_sb[:, 0:1])
    nc.const_aps.register(F32, float(EPS_RMS), consts_sb[:, 1:2])
    nc.const_aps.register(F32, 1.0 / D, consts_sb[:, 2:3])

    dq_pack = sb("dq_pack", [128, 2, 128], I32)
    dq_nib = sb("dq_nib", [128, DSEC], I32)
    dq_mi = sb("dq_mi", [128, DSEC], I32)
    dq_mf = sb("dq_mf", [128, DSEC], F32)
    dq_sf = sb("dq_sf", [128, DSEC], F32)
    dq_t1 = sb("dq_t1", [128, DSEC], F32)
    dq_v = sb("dq_v", [128, 2, DSEC], F32)
    dq_sc16 = sb("dq_sc16", [128, 2, 8], F16)
    dq_sc = sb("dq_sc", [128, 2, 8], F32)
    dq_out = sb("dq_out", [128, 8, 128], BF16)

    wg_buf = sb("wg_buf", [128, 2, 2, NDC, 128], BF16)
    wd_buf = sb("wd_buf", [128, 2, 4, 512], BF16)
    silu_t = sb("silu_t", [128, 4, 512], BF16)
    rms_s = sb("rms_s", [128, 16], F32)
    rms_c = sb("rms_c", [128, 8], F32)
    pswap_sb = sb("pswap_sb", [128, 128], BF16)
    causal_sb = sb("causal_sb", [128, 128], F32)
    colmask_sb = sb("colmask_sb", [128, 512], F32)
    ownsel_sb = sb("ownsel_sb", [128, 512], F32)
    diag_sb = sb("diag_sb", [128, 8], F32)
    tA_sb = sb("tA_sb", [128, 296], F32)

    bm_own = sb("bm_own", [128, NKV, 8], F32)
    bm_sb = sb("bm_sb", [128, NKV, 64], F32)
    bm_sbb = sb("bm_sbb", [128, NKV, 64], BF16)
    notF_sb = sb("notF_sb", [128, NH, 8], F32)
    recip_sb = sb("recip_sb", [128, NH + 2], F32)
    kbuf = sb("kbuf", [128, 2, NKV, 128], BF16)
    vbuf = sb("vbuf", [128, 2, NKV, 129], BF16)
    probs_sb = sb("probs_sb", [128, 2, 512], BF16)
    ropeC_sb = dq_t1
    ropeS_sb = None
    wv_alias = bass.AP(h_T, 0, [[h_T.shape[1] * h_T.shape[2], 128], [1024, 16], [1, 1024]])
    ropeS_sb = bass.AP(dq_v, 0, [[2 * DSEC, 128], [1, ROWS]])

    pb = [ctx.enter_context(nc.psum_tensor(f"pb{i}", [128, 512], F32)) for i in range(8)]

    sems = {}
    for name in SEM_NAMES:
        h = ctx.enter_context(nc.semaphore(name))
        sems[name] = [h, 0]

    blk = ctx.enter_context(nc.Block())

    def full_sync(em):
        for role in ("sync", "scalar", "vector", "tensor", "gpsimd"):
            for s in ("qx", "qw", "qg", "pe", "act", "dve"):
                em.wait(role, s)

    def ln_tile(em, src, qt, lq, gen, tap=False):
        slot = qt % 2
        em.wait("sync", "act")
        em.wait("sync", "qg")
        em.dma("qx", xstage[:, slot, :], src.rearrange("(q p) d -> q p d", p=128)[qt, :, :], "qx")
        em.wait("scalar", "qx")
        em.wait("scalar", "pe")
        em.doi("scalar", "act",
               lambda e: e.activation(out=ln_tok[:, :], in_=xstage[:, slot, :], func=AF.Square,
                                      accum_out=ln_s[:, 1, 0:1]))
        em.wait("vector", "act")
        em.doi("vector", "dve", lambda e: e.tensor_reduce(ln_s[:, 0, 0:1], xstage[:, slot, :], axis=AX, op=OP.add))
        em.wait("vector", "dve")
        em.doi("vector", "dve", lambda e: e.tensor_scalar(ln_s[:, 2, 0:1], ln_s[:, 0, 0:1], 1.0 / D, None, OP.mult))
        em.wait("vector", "dve")
        em.doi("vector", "dve", lambda e: e.tensor_tensor(ln_s[:, 3, 0:1], ln_s[:, 2, 0:1], ln_s[:, 2, 0:1], OP.mult))
        em.wait("vector", "dve")
        em.doi("vector", "dve", lambda e: e.scalar_tensor_tensor(ln_s[:, 3, 0:1], ln_s[:, 1, 0:1], 1.0 / D,
                                                                 ln_s[:, 3, 0:1], OP.mult, OP.subtract))
        em.wait("scalar", "dve")
        em.doi("scalar", "act", lambda e: e.activation(out=ln_rstd[:, 1:2], in_=ln_s[:, 3, 0:1], func=AF.Sqrt,
                                                       bias=float(EPS_LN)))
        em.wait("vector", "act")
        em.doi("vector", "dve", lambda e: e.reciprocal(ln_rstd[:, 0:1], ln_rstd[:, 1:2]))
        em.wait("vector", "dve")
        em.doi("vector", "dve", lambda e: e.tensor_tensor(ln_mb[:, 0:1], ln_s[:, 2, 0:1], ln_rstd[:, 0:1], OP.mult))
        em.wait("vector", "dve")
        em.doi("vector", "dve", lambda e: e.tensor_scalar(ln_mb[:, 0:1], ln_mb[:, 0:1], -1.0, None, OP.mult))
        em.wait("scalar", "dve")
        em.doi("scalar", "act",
               lambda e: e.activation(out=ln_tok[:, :], in_=xstage[:, slot, :], func=AF.Identity,
                                      scale=ln_rstd[:, 0:1], bias=ln_mb[:, 0:1]))
        em.wait("tensor", "act")
        for b in range(4):
            for k in range(4):
                dch = 4 * b + k
                em.doi("tensor", "pe",
                       lambda e, dch=dch, k=k: e.transpose(pb[4][:, k * 128:(k + 1) * 128],
                                                           ln_tok[:, dch * 128:(dch + 1) * 128], ident[:, :]))
            em.wait("vector", "pe")
            em.doi("vector", "dve",
                   lambda e, b=b, lq=lq: e.tensor_copy(
                       x_lnT[:, 4 * b:4 * b + 4, lq * 128:(lq + 1) * 128],
                       pb[4][:, :].rearrange("p (a b) -> p a b", a=4)))
            em.wait("tensor", "dve")

    def dequant_ag(em):
        gu3 = gu_packed.rearrange("(n p) w -> n p w", p=128)
        gs3 = gu_scales.rearrange("(n p) w -> n p w", p=128)
        dn3 = dn_packed.rearrange("(n p) w -> n p w", p=128)
        ds3 = dn_scales.rearrange("(n p) w -> n p w", p=128)

        dq_state = {"si": 0, "ext": [], "tr": []}

        def decode_sec(packed3, rt, w_off, nw, scales3, g_off, ngrp, store_fn):
            si = dq_state["si"]; dq_state["si"] = si + 1
            par = si % 2
            W = nw * 8
            if si >= 2:
                em.wait("sync", "dve", dq_state["ext"][si - 2])
            em.dma("qx", dq_pack[:, par, 0:nw], packed3[rt, :, w_off:w_off + nw], "qx")
            em.dma("qx", dq_sc16[:, par, 0:ngrp], scales3[rt, :, g_off:g_off + ngrp], "qx")
            qgm = em.val("qx")
            em.wait("vector", "qx", qgm)
            em.doi("vector", "dve", lambda e: e.tensor_copy(dq_sc[:, par, 0:ngrp], dq_sc16[:, par, 0:ngrp]))
            for k in range(8):
                em.doi("vector", "dve",
                       lambda e, k=k: e.tensor_scalar(dq_nib[:, k:W:8], dq_pack[:, par, 0:nw], 4 * k, 15,
                                                      OP.logical_shift_right, OP.bitwise_and))
            dq_state["ext"].append(em.val("dve"))
            em.doi("vector", "dve", lambda e: e.tensor_scalar(dq_mi[:, 0:W], dq_nib[:, 0:W], 7, None, OP.bitwise_and))
            em.doi("vector", "dve", lambda e: e.tensor_copy(dq_mf[:, 0:W], dq_mi[:, 0:W]))
            em.doi("vector", "dve", lambda e: e.tensor_scalar(dq_mi[:, 0:W], dq_nib[:, 0:W], 3, None,
                                                              OP.logical_shift_right))
            em.doi("vector", "dve", lambda e: e.tensor_copy(dq_sf[:, 0:W], dq_mi[:, 0:W]))
            em.doi("vector", "dve", lambda e: e.tensor_scalar(dq_t1[:, 0:W], dq_mf[:, 0:W], -4.0, 0.0, OP.add, OP.max))
            if si >= 2:
                em.wait("vector", "pe", dq_state["tr"][si - 2])
            em.doi("vector", "dve", lambda e: e.tensor_tensor(dq_v[:, par, 0:W], dq_mf[:, 0:W], dq_t1[:, 0:W], OP.add))
            em.doi("vector", "dve", lambda e: e.tensor_scalar(dq_t1[:, 0:W], dq_mf[:, 0:W], -6.0, 0.0, OP.add, OP.max))
            em.doi("vector", "dve", lambda e: e.scalar_tensor_tensor(dq_v[:, par, 0:W], dq_v[:, par, 0:W], 0.5,
                                                                     dq_t1[:, 0:W], OP.mult, OP.add))
            em.doi("vector", "dve", lambda e: e.tensor_scalar(dq_sf[:, 0:W], dq_sf[:, 0:W], -2.0, 1.0, OP.mult, OP.add))
            em.doi("vector", "dve", lambda e: e.tensor_tensor(dq_v[:, par, 0:W], dq_v[:, par, 0:W], dq_sf[:, 0:W], OP.mult))
            for g in range(ngrp):
                em.doi("vector", "dve",
                       lambda e, g=g: e.tensor_scalar(dq_v[:, par, g * 128:(g + 1) * 128],
                                                      dq_v[:, par, g * 128:(g + 1) * 128],
                                                      dq_sc[:, par, g:g + 1], None, OP.mult))
            em.wait("tensor", "dve")
            em.wait("scalar", "qg")
            for b in range((ngrp + 3) // 4):
                kk = min(4, ngrp - 4 * b)
                for k in range(kk):
                    em.doi("tensor", "pe",
                           lambda e, b=b, k=k: e.transpose(pb[5][:, k * 128:(k + 1) * 128],
                                                           dq_v[:, par, (4 * b + k) * 128:(4 * b + k + 1) * 128],
                                                           ident[:, :]))
                em.wait("scalar", "pe")
                for k in range(kk):
                    em.doi("scalar", "act",
                           lambda e, b=b, k=k: e.copy(out=dq_out[:, 4 * b + k, :],
                                                      in_=pb[5][:, k * 128:(k + 1) * 128]))
                em.wait("tensor", "act")
            dq_state["tr"].append(em.val("pe"))
            em.wait("gpsimd", "act")
            for j in range(ngrp):
                store_fn(em, j)

        def mk_gu_store2(rt, s):
            k6, is_up = rt % 6, rt // 6
            def store(em, j):
                dch = 8 * s + j
                dst = agw_in[k6].ap()[is_up * 128:(is_up + 1) * 128, dch * 128:(dch + 1) * 128]
                em.dma("qg", dst, dq_out[:, j, :], "qg")
            return store

        for k6 in range(6):
            for rt in (k6, 6 + k6):
                for s in range(2):
                    decode_sec(gu3, rt, 128 * s, 128, gs3, 8 * s, 8, mk_gu_store2(rt, s))
            em.wait("gpsimd", "qg")
            em.doi("gpsimd", f"ccr{k6}",
                   lambda e, k6=k6: e.collective_compute("AllGather", OP.bypass,
                                                         replica_groups=[list(range(NCORE))],
                                                         ins=[agw_in[k6].ap().opt()],
                                                         outs=[agw_out[k6].ap().opt()]))

        def mk_dn_store(rt, s):
            def store(em, j):
                ich = 8 * s + j
                dst = agd_in.ap()[ich * 128:(ich + 1) * 128, rt * 128:(rt + 1) * 128]
                em.dma("qg", dst, dq_out[:, j, :], "qg")
            return store

        for rt in range(2):
            for s in range(6):
                ngrp = min(8, 44 - 8 * s)
                decode_sec(dn3, rt, 128 * s, min(128, 704 - 128 * s), ds3, 8 * s, ngrp, mk_dn_store(rt, s))

        em.wait("gpsimd", "qg")
        em.doi("gpsimd", "ccd",
               lambda e: e.collective_compute("AllGather", OP.bypass, replica_groups=[list(range(NCORE))],
                                              ins=[agd_in.ap().opt()], outs=[agd_out.ap().opt()]))
        for s in ("ccr0", "ccr1", "ccr2", "ccr3", "ccr4", "ccr5", "ccd"):
            em.wait("gpsimd", s, 1)
        em.doi("gpsimd", "ph", lambda e: e.engine_nop())
        for r in ("sync", "scalar", "vector", "tensor"):
            em.wait(r, "ph", em.val("ph"))

    def ffn(em, src, dst, half_scale, cc_base=None):
        chunks = [(k, c2) for k in range(6) for c2 in range(NCORE) if 8 * k + c2 < NIC]
        for half in range(2):
            for lq in range(4):
                ln_tile(em, src, half * 4 + lq, lq, 0, tap=(stage == "dbg" and half == 0 and lq == 0))
            em.wait("sync", "pe")
            em.wait("tensor", "dve")
            em.wait("scalar", "dve")
            pe_marks = []
            dve_marks1 = []
            for ci, (k, c2) in enumerate(chunks):
                g = 8 * k + c2
                buf = ci % 2
                if ci >= 2:
                    em.wait("sync", "pe", pe_marks[ci - 2])
                if cc_base is not None:
                    em.wait("sync", f"ccr{k}", 1)
                    if k < 5:
                        em.wait("sync", f"ccr{k + 1}", 1)
                    else:
                        em.wait("sync", "ccd", 1)
                base = agw_out[k].ap()
                em.dma("qw", wg_buf[:, buf, 0, :, :].rearrange("p a b -> p (a b)"),
                       base[c2 * 256:c2 * 256 + 128, :], "qw")
                em.dma("qw", wg_buf[:, buf, 1, :, :].rearrange("p a b -> p (a b)"),
                       base[c2 * 256 + 128:c2 * 256 + 256, :], "qw")
                em.wait("tensor", "qw")
                if ci >= 2:
                    em.wait("tensor", "dve", dve_marks1[ci - 2])
                pg, pu = pb[2 * buf], pb[2 * buf + 1]
                for dch in range(NDC):
                    em.doi("tensor", "pe",
                           lambda e, dch=dch, buf=buf, pg=pg: e.matmul(pg[:, :], wg_buf[:, buf, 0, dch, :],
                                                                       x_lnT[:, dch, :],
                                                                       start=(dch == 0), stop=(dch == NDC - 1)))
                for dch in range(NDC):
                    em.doi("tensor", "pe",
                           lambda e, dch=dch, buf=buf, pu=pu: e.matmul(pu[:, :], wg_buf[:, buf, 1, dch, :],
                                                                       x_lnT[:, dch, :],
                                                                       start=(dch == 0), stop=(dch == NDC - 1)))
                pe_marks.append(em.val("pe"))
                em.wait("scalar", "pe")
                if ci >= 2:
                    em.wait("scalar", "dve", dve_marks1[ci - 2])
                em.doi("scalar", "act", lambda e, pg=pg, ci=ci: e.activation(out=silu_t[:, ci % 2, :], in_=pg[:, :], func=AF.Silu))
                em.wait("vector", "act")
                em.doi("vector", "dve",
                       lambda e, g=g, ci=ci, pu=pu: e.tensor_tensor(h_T[:, g, :], silu_t[:, ci % 2, :], pu[:, :], OP.mult))
                dve_marks1.append(em.val("dve"))

            em.wait("tensor", "dve")
            for pair in range(2):
                qts = (half * 4 + pair * 2, half * 4 + pair * 2 + 1)
                dve_marks = []
                em.wait("sync", "pe")
                pe_marks2 = []
                for ic in range(NIC):
                    buf = ic % 2
                    if ic >= 2:
                        em.wait("sync", "pe", pe_marks2[ic - 2])
                    em.dma("qw", wd_buf[:, buf, :, :].rearrange("p a b -> p (a b)"),
                           bass.AP(agd_out, ic * 128 * 256,
                                   [[256, 128], [IH * 256, 8], [1, 256]]), "qw")
                    em.wait("tensor", "qw")
                    for qi in range(2):
                        lq = qts[qi] - half * 4
                        for dg in range(4):
                            em.doi("tensor", "pe",
                                   lambda e, ic=ic, buf=buf, qi=qi, lq=lq, dg=dg: e.matmul(
                                       pb[qi * 4 + dg][:, :], h_T[:, ic, lq * 128:(lq + 1) * 128],
                                       wd_buf[:, buf, dg, :],
                                       start=(ic == 0), stop=(ic == NIC - 1)))
                    pe_marks2.append(em.val("pe"))
                em.wait("scalar", "pe")
                for qi in range(2):
                    for dg in range(4):
                        em.doi("scalar", "act",
                               lambda e, qi=qi, dg=dg: e.activation(out=ln_tok[:, 0:512], in_=pb[qi * 4 + dg][:, :],
                                                                    func=AF.Square,
                                                                    accum_out=rms_s[:, qi * 8 + dg:qi * 8 + dg + 1]))
                for qi in range(2):
                    em.doi("scalar", "act",
                           lambda e, qi=qi: e.copy(out=rms_c[:, qi * 4:qi * 4 + 4], in_=rms_s[:, qi * 8:qi * 8 + 4]))
                em.wait("vector", "act")
                for qi in range(2):
                    em.doi("vector", "dve",
                           lambda e, qi=qi: e.tensor_reduce(rms_s[:, qi * 8 + 4:qi * 8 + 5],
                                                            rms_c[:, qi * 4:qi * 4 + 4], axis=AX, op=OP.add))
                em.wait("vector", "dve")
                em.wait("scalar", "dve")
                for qi in range(2):
                    em.doi("scalar", "act",
                           lambda e, qi=qi: e.activation(out=rms_s[:, qi * 8 + 5:qi * 8 + 6],
                                                         in_=rms_s[:, qi * 8 + 4:qi * 8 + 5],
                                                         func=AF.Sqrt, bias=float(EPS_RMS), scale=1.0 / D))
                em.wait("vector", "act")
                for qi in range(2):
                    em.doi("vector", "dve",
                           lambda e, qi=qi: e.reciprocal(rms_s[:, qi * 8 + 7:qi * 8 + 8],
                                                         rms_s[:, qi * 8 + 5:qi * 8 + 6]))
                    em.wait("vector", "dve")
                    em.doi("vector", "dve",
                           lambda e, qi=qi: e.tensor_scalar(rms_s[:, qi * 8 + 6:qi * 8 + 7],
                                                            rms_s[:, qi * 8 + 7:qi * 8 + 8],
                                                            float(half_scale), None, OP.mult))
                em.wait("sync", "act")
                em.wait("sync", "qg")
                for qi in range(2):
                    qt = qts[qi]
                    em.dma("qx", xstage[:, qi, :], src.rearrange("(q p) d -> q p d", p=128)[qt, :, :], "qx")
                em.wait("vector", "qx")
                em.doi("vector", "ph", lambda e: e.engine_nop())
                em.wait("vector", "ph")
                for qi in range(2):
                    for dg in range(4):
                        em.doi("vector", "dve",
                               lambda e, qi=qi, dg=dg: e.scalar_tensor_tensor(
                                   xstage[:, qi, dg * 512:(dg + 1) * 512], pb[qi * 4 + dg][:, :],
                                   rms_s[:, qi * 8 + 6:qi * 8 + 7], xstage[:, qi, dg * 512:(dg + 1) * 512],
                                   OP.mult, OP.add))
                em.wait("gpsimd", "dve")
                for qi in range(2):
                    qt = qts[qi]
                    em.dma("qg", dst.rearrange("(q p) d -> q p d", p=128)[qt, :, :], xstage[:, qi, :], "qg")
                em.wait("tensor", "qg")
                em.wait("tensor", "dve")


    HT_ROW = NIC * 512
    QS_ROW = NQT * NH * 128

    def wv_ap(dch):
        return bass.AP(h_T, dch * 1024, [[HT_ROW, 128], [1, 1024]])

    def wo_ap(dch, lo, n):
        if dch < 11:
            return bass.AP(h_T, dch * 2048 + lo, [[HT_ROW, 128], [1, n]])
        if dch < 15:
            return bass.AP(wg_buf, (dch - 11) * 2048 + lo, [[8192, 128], [1, n]])
        return bass.AP(x_lnT, lo, [[8192, 128], [1, n]])

    def attnT_ap(dch):
        return bass.AP(x_lnT, 2048 + dch * 128, [[8192, 128], [1, 128]])

    def acc_ap(qh, lo, n):
        return pb[qh // 3][:, (qh % 3) * 129 + lo:(qh % 3) * 129 + lo + n]

    def load_consts_attn(em):
        em.dma("qx", pswap_sb[:, :], pswap_in, "qx")
        em.dma("qx", causal_sb[:, :], causal_in, "qx")
        em.dma("qx", colmask_sb[:, :], colmask_in, "qx")
        em.dma("qx", ownsel_sb[:, :], ownsel_in, "qx")
        em.dma("qx", diag_sb[:, :], diag_in, "qx")
        em.dma("qx", tA_sb[:, :], tA_in, "qx")

        em.wait("vector", "qx")
        for s in range(2):
            em.doi("vector", "dve", lambda e, s=s: e.memset(vbuf[:, s, :, 128:129], 1.0))

    def proj_attn(em, src):
        em.wait("sync", "dve")
        em.wait("sync", "act")
        em.dma("qx", ropeC_sb[:, :], ropeC_in, "qx")
        em.dma("qx", ropeS_sb[:, :], ropeS_in, "qx")
        for half in range(2):
            for lq in range(4):
                ln_tile(em, src, half * 4 + lq, lq, 1)
            if half == 0:
                for dch in range(NDC):
                    em.wait("sync", "dve")
                    em.dma("qw", dq_mf[:, :], wv_in.rearrange("(a p) n -> a p n", p=128)[dch, :, :], "qw")
                    em.wait("vector", "qw")
                    em.doi("vector", "dve", lambda e, dch=dch: e.tensor_copy(wv_ap(dch), dq_mf[:, :]))
                em.wait("tensor", "dve")
            for tt in range(4):
                em.wait("tensor", "act")
                for dch in range(NDC):
                    em.doi("tensor", "pe",
                           lambda e, dch=dch, tt=tt: e.matmul(pb[2][:, :], x_lnT[:, dch, tt * 128:(tt + 1) * 128],
                                                              wv_ap(dch)[:, 0:512],
                                                              start=(dch == 0), stop=(dch == NDC - 1)))
                for dch in range(NDC):
                    em.doi("tensor", "pe",
                           lambda e, dch=dch, tt=tt: e.matmul(pb[3][:, :], x_lnT[:, dch, tt * 128:(tt + 1) * 128],
                                                              wv_ap(dch)[:, 512:1024],
                                                              start=(dch == 0), stop=(dch == NDC - 1)))
                em.wait("scalar", "pe")
                em.wait("scalar", "qg")
                em.doi("scalar", "act", lambda e: e.copy(out=silu_t[:, 0, :], in_=pb[2][:, :]))
                em.doi("scalar", "act", lambda e: e.copy(out=silu_t[:, 1, :], in_=pb[3][:, :]))
                em.wait("gpsimd", "act")
                gt = (half * 4 + tt) * 128
                em.dma("qg", vin.ap()[gt:gt + 128, 0:512], silu_t[:, 0, :], "qg")
                em.dma("qg", vin.ap()[gt:gt + 128, 512:1024], silu_t[:, 1, :], "qg")
            heads = [("k", j) for j in range(NKV)] + [("q", j) for j in range(NH)]
            NHD = len(heads)
            mm_mark = [0] * NHD; conv_mark = [0] * NHD; evac_mark = [0] * NHD
            swap_mark = [0] * NHD; rope_mark = [0] * NHD; qwm = [0] * NHD

            def stage_ap(par):
                return (ln_tok if par == 0 else wstage2)[:, :]

            def wkb_ap(par, dch, n=128):
                return bass.AP(wg_buf, par * 4096 + dch * 128, [[8192, 128], [1, n]])

            def emit_swap(h2):
                kind2, j2 = heads[h2]; par2 = h2 % 2
                ps2 = pb[1] if par2 == 0 else pb[3]
                em.wait("tensor", "dve", conv_mark[h2])
                if h2 >= 2:
                    em.wait("tensor", "dve", rope_mark[h2 - 2])
                em.doi("tensor", "pe", lambda e, ps2=ps2, par2=par2: e.matmul(
                    ps2[:, :], pswap_sb[:, :], silu_t[:, par2, :], start=True, stop=True))
                swap_mark[h2] = em.val("pe")

            def emit_rope(h2):
                kind2, j2 = heads[h2]; par2 = h2 % 2
                ps2 = pb[1] if par2 == 0 else pb[3]
                em.wait("vector", "pe", swap_mark[h2])
                em.doi("vector", "dve",
                       lambda e, half=half, par2=par2: e.tensor_tensor(
                           dq_sf[:, 0:512], dq_mf[:, par2 * 512:par2 * 512 + 512],
                           ropeC_sb[:, half * 512:(half + 1) * 512], OP.mult))
                em.doi("vector", "dve",
                       lambda e, half=half, ps2=ps2: e.tensor_tensor(
                           dq_sf[:, 512:1024], ps2[:, :],
                           ropeS_sb[:, half * 512:(half + 1) * 512], OP.mult))
                em.wait("vector", "dve")
                if kind2 == "k":
                    em.wait("vector", "qg")
                    em.doi("vector", "dve",
                           lambda e: e.tensor_tensor(silu_t[:, 2, :], dq_sf[:, 0:512], dq_sf[:, 512:1024], OP.add))
                    em.wait("vector", "dve")
                    em.doi("vector", "dve",
                           lambda e, j2=j2, half=half: e.tensor_reduce(
                               bm_own[:, j2, half * 4:half * 4 + 4],
                               silu_t[:, 2, :].rearrange("p (a b) -> p a b", a=4), axis=AX, op=OP.add))
                    em.wait("gpsimd", "dve")
                    em.dma("qg", kin.ap()[j2 * 128:(j2 + 1) * 128, half * 512:(half + 1) * 512],
                           silu_t[:, 2, :], "qg")
                else:
                    em.doi("vector", "dve",
                           lambda e: e.tensor_tensor(dq_sf[:, 0:512], dq_sf[:, 0:512], dq_sf[:, 512:1024], OP.add))
                    em.wait("vector", "dve")
                    em.doi("vector", "dve",
                           lambda e, j2=j2, half=half: e.tensor_scalar(
                               bass.AP(q_sb, (half * 4) * 2048 + j2 * 128, [[QS_ROW, 128], [2048, 4], [1, 128]]),
                               dq_sf[:, 0:512].rearrange("p (a b) -> p a b", a=4),
                               float(1.0 / math.sqrt(HD)), None, OP.mult))
                rope_mark[h2] = em.val("dve")

            for hi, (kind, j) in enumerate(heads):
                par = hi % 2
                w_in = wk_in if kind == "k" else wq_in
                pq = pb[0] if par == 0 else pb[2]
                if hi >= 2:
                    em.wait("sync", "pe", mm_mark[hi - 2])
                    em.wait("sync", "dve", conv_mark[hi - 2])
                elif hi == 0:
                    em.wait("sync", "pe")
                for dch in range(NDC):
                    em.dma("qw", stage_ap(par)[:, dch * 128:(dch + 1) * 128],
                           w_in[dch * 128:(dch + 1) * 128, j * 128:(j + 1) * 128], "qw")
                qwm[hi] = em.val("qw")
                em.wait("vector", "qw", qwm[hi])
                if hi >= 2:
                    em.wait("vector", "pe", mm_mark[hi - 2])
                em.doi("vector", "dve", lambda e, par=par: e.tensor_copy(wkb_ap(par, 0, 2048), stage_ap(par)))
                conv_mark[hi] = em.val("dve")
                em.wait("tensor", "dve", conv_mark[hi])
                if hi >= 2:
                    em.wait("tensor", "act", evac_mark[hi - 2])
                for dch in range(NDC):
                    em.doi("tensor", "pe",
                           lambda e, dch=dch, par=par, pq=pq: e.matmul(pq[:, :], wkb_ap(par, dch),
                                                                      x_lnT[:, dch, :],
                                                                      start=(dch == 0), stop=(dch == NDC - 1)))
                mm_mark[hi] = em.val("pe")
                em.wait("scalar", "pe", mm_mark[hi])
                if hi >= 2:
                    em.wait("scalar", "dve", rope_mark[hi - 2])
                else:
                    em.wait("scalar", "dve")
                em.doi("scalar", "act", lambda e, pq=pq, par=par: e.copy(out=dq_mf[:, par * 512:par * 512 + 512],
                                                                        in_=pq[:, :]))
                evac_mark[hi] = em.val("act")
                em.wait("vector", "act", evac_mark[hi])
                if hi >= 2:
                    em.wait("vector", "pe", swap_mark[hi - 2])
                else:
                    em.wait("vector", "qg")
                em.doi("vector", "dve", lambda e, par=par: e.tensor_copy(silu_t[:, par, :],
                                                                         dq_mf[:, par * 512:par * 512 + 512]))
                conv_mark[hi] = em.val("dve")
                if hi >= 1:
                    emit_swap(hi - 1)
                    emit_rope(hi - 1)
            emit_swap(NHD - 1)
            emit_rope(NHD - 1)
        em.wait("gpsimd", "dve")
        for kvh in range(NKV):
            em.dma("qg", bmin.ap()[kvh * 128:(kvh + 1) * 128, :],
                   bm_own[:, kvh, :], "qg")

    def ag_kv(em):
        em.wait("gpsimd", "qg")
        em.doi("gpsimd", "ccb",
               lambda e: e.collective_compute("AllGather", OP.bypass, replica_groups=[list(range(NCORE))],
                                              ins=[bmin.ap().opt()], outs=[bmout.ap().opt()]))
        em.doi("gpsimd", "cck",
               lambda e: e.collective_compute("AllGather", OP.bypass, replica_groups=[list(range(NCORE))],
                                              ins=[kin.ap().opt()], outs=[kout.ap().opt()]))
        em.doi("gpsimd", "ccv",
               lambda e: e.collective_compute("AllGather", OP.bypass, replica_groups=[list(range(NCORE))],
                                              ins=[vin.ap().opt()], outs=[vout.ap().opt()]))
        return 0

    def wo_load(em):
        for dch in range(NDC):
            em.wait("sync", "dve")
            em.dma("qw", ln_tok[:, :], wo_in.rearrange("(a p) n -> a p n", p=128)[dch, :, :], "qw")
            em.wait("vector", "qw")
            em.doi("vector", "dve", lambda e, dch=dch: e.tensor_copy(wo_ap(dch, 0, 2048), ln_tok[:, :]))

    def gating(em, cc_bm):
        em.wait("sync", "ccb", 1)
        for kvh in range(NKV):
            em.dma("qx", bm_sb[:, kvh, :],
                   bass.AP(bmout, kvh * 128 * 8, [[8, 128], [8192, 8], [1, 8]]), "qx")
        em.wait("vector", "qx")
        em.doi("vector", "dve", lambda e: e.tensor_copy(bm_sbb[:, :, :], bm_sb[:, :, :]))
        em.wait("tensor", "dve")
        for qh in range(NH):
            em.wait("tensor", "dve")
            for i in range(NQT):
                em.doi("tensor", "pe",
                       lambda e, i=i, qh=qh: e.matmul(pb[6][:, i * 64:(i + 1) * 64],
                                                      bass.AP(q_sb, i * 2048 + qh * 128, [[QS_ROW, 128], [1, 128]]),
                                                      bm_sbb[:, qh // 2, :], start=True, stop=True))
            em.wait("vector", "pe")
            em.doi("vector", "dve", lambda e: e.tensor_copy(ln_tok[:, 0:512], pb[6][:, :]))
            em.wait("vector", "dve")
            em.doi("vector", "dve", lambda e: e.tensor_tensor(ln_tok[:, 512:1024], ln_tok[:, 0:512],
                                                              ownsel_sb[:, :], OP.mult))
            em.wait("vector", "dve")
            em.doi("vector", "dve", lambda e: e.tensor_reduce(rms_s[:, 0:8],
                                                              ln_tok[:, 512:1024].rearrange("p (a b) -> p a b", a=8),
                                                              axis=AX, op=OP.add))
            em.wait("vector", "dve")
            em.doi("vector", "dve", lambda e: e.tensor_tensor(ln_tok[:, 512:1024], ln_tok[:, 0:512],
                                                              bass.AP(rms_s, 0, [[16, 128], [1, 8], [0, 64]]),
                                                              OP.is_gt))
            em.wait("vector", "dve")
            em.doi("vector", "dve", lambda e: e.tensor_tensor(ln_tok[:, 512:1024], ln_tok[:, 512:1024],
                                                              colmask_sb[:, :], OP.mult))
            em.wait("vector", "dve")
            em.doi("vector", "dve", lambda e: e.tensor_reduce(rms_s[:, 8:16],
                                                              ln_tok[:, 512:1024].rearrange("p (a b) -> p a b", a=8),
                                                              axis=AX, op=OP.add))
            em.wait("vector", "dve")
            em.doi("vector", "dve", lambda e: e.tensor_scalar(rms_s[:, 0:8], rms_s[:, 8:16], 7.5, None, OP.is_gt))
            em.wait("vector", "dve")
            em.doi("vector", "dve", lambda e, qh=qh: e.tensor_tensor(notF_sb[:, qh, :], rms_s[:, 0:8],
                                                                     diag_sb[:, :], OP.mult))
        em.wait("vector", "dve") if False else None

    def flash(em, dst):
        src_attn = xres_hbm.ap()
        em.wait("sync", "cck", 1)
        em.wait("sync", "ccv", 1)
        step_idx = 0
        pe_marks_f = []
        bank_act = [0, 0]
        bank_dve = [0, 0]
        probs_pe = [0, 0]
        for i in range(NQT):
            nsteps = 8 * i + 9
            em.wait("tensor", "dve")
            for st in range(nsteps):
                buf = step_idx % 2
                if step_idx >= 2:
                    em.wait("sync", "pe", pe_marks_f[step_idx - 2])
                if st == 0:
                    koff, kt = i * 128, kin
                    voff, vt = (i * 128) * 1024, vin
                else:
                    g = st - 1
                    c2, slot = g % 8, g // 8
                    koff, kt = c2 * 1024 * 1024 + slot * 128, kout
                    voff, vt = (c2 * 1024 + slot * 128) * 1024, vout
                em.dma("qf", kbuf[:, buf, :, :],
                       bass.AP(kt, koff, [[1024, 128], [131072, 8], [1, 128]]), "qf")
                em.dma("qf", vbuf[:, buf, :, 0:128],
                       bass.AP(vt, voff, [[1024, 128], [128, 8], [1, 128]]), "qf")
                em.wait("tensor", "qf")
                if st == 0:
                    for p in range(4):
                        em.wait("tensor", "act", bank_act[0])
                        for j in range(4):
                            qh = 4 * p + j
                            em.doi("tensor", "pe",
                                   lambda e, i=i, qh=qh, j=j, buf=buf: e.matmul(
                                       pb[6][:, j * 128:(j + 1) * 128],
                                       bass.AP(q_sb, i * 2048 + qh * 128, [[QS_ROW, 128], [1, 128]]),
                                       kbuf[:, buf, qh // 2, :], start=True, stop=True))
                        em.wait("vector", "pe")
                        for j in range(4):
                            qh = 4 * p + j
                            em.doi("vector", "dve",
                                   lambda e, qh=qh, j=j, i=i: e.scalar_tensor_tensor(
                                       pb[6][:, j * 128:(j + 1) * 128], causal_sb[:, :],
                                       notF_sb[:, qh, i:i + 1], pb[6][:, j * 128:(j + 1) * 128],
                                       OP.mult, OP.add))
                        em.wait("scalar", "dve")
                        em.doi("scalar", "act",
                               lambda e, s=step_idx: e.activation(out=dq_mf[:, 0:512], in_=pb[6][:, :],
                                                                  func=AF.Exp, bias=tA_sb[:, s:s + 1]))
                        bank_act[0] = em.val("act")
                        em.wait("tensor", "act")
                        em.wait("tensor", "act", bank_act[1])
                        for j in range(4):
                            em.doi("tensor", "pe",
                                   lambda e, j=j: e.transpose(pb[7][:, j * 128:(j + 1) * 128],
                                                              dq_mf[:, j * 128:(j + 1) * 128], ident[:, :]))
                        em.wait("vector", "pe")
                        em.wait("vector", "pe", probs_pe[p % 2])
                        em.doi("vector", "dve",
                               lambda e, p=p: e.tensor_copy(probs_sb[:, p % 2, :], pb[7][:, :]))
                        bank_dve[1] = em.val("dve")
                        em.wait("tensor", "dve")
                        for j in range(4):
                            qh = 4 * p + j
                            em.doi("tensor", "pe",
                                   lambda e, p=p, j=j, qh=qh, buf=buf: e.matmul(
                                       acc_ap(qh, 0, 129),
                                       probs_sb[:, p % 2, j * 128:(j + 1) * 128],
                                       vbuf[:, buf, qh // 2, :],
                                       start=(qh % 3 == 0), stop=False))
                        probs_pe[p % 2] = em.val("pe")
                else:
                    exp_marks = [None] * 4
                    def emit_pv(p, buf=buf, st=st, nsteps=nsteps):
                        b = p % 2
                        em.wait("tensor", "act", exp_marks[p])
                        for j in range(2):
                            for qq in range(2):
                                qh = 2 * (2 * p + j) + qq
                                em.doi("tensor", "pe",
                                       lambda e, b=b, j=j, qq=qq, qh=qh, buf=buf, st=st, nsteps=nsteps: e.matmul(
                                           acc_ap(qh, 0, 129),
                                           probs_sb[:, b, (j * 2 + qq) * 128:(j * 2 + qq + 1) * 128],
                                           vbuf[:, buf, qh // 2, :], start=False,
                                           stop=(st == nsteps - 1 and (qh % 3 == 2 or qh == 15))))
                        probs_pe[b] = em.val("pe")
                    for p in range(4):
                        b = p % 2
                        em.wait("tensor", "act", bank_act[b])
                        em.wait("tensor", "dve", bank_dve[b])
                        for j in range(2):
                            kvh = 2 * p + j
                            em.doi("tensor", "pe",
                                   lambda e, i=i, kvh=kvh, j=j, buf=buf, b=b: e.matmul(
                                       pb[6 + b][:, j * 256:(j + 1) * 256],
                                       kbuf[:, buf, kvh, :],
                                       bass.AP(q_sb, i * 2048 + 2 * kvh * 128, [[QS_ROW, 128], [1, 256]]),
                                       start=True, stop=True))
                        scores_mark = em.val("pe")
                        em.wait("scalar", "pe", scores_mark)
                        em.wait("scalar", "pe", probs_pe[b])
                        em.doi("scalar", "act",
                               lambda e, b=b, s=step_idx: e.activation(out=probs_sb[:, b, :], in_=pb[6 + b][:, :],
                                                                       func=AF.Exp, bias=tA_sb[:, s:s + 1]))
                        bank_act[b] = em.val("act")
                        exp_marks[p] = em.val("act")
                        if p >= 1:
                            emit_pv(p - 1)
                    emit_pv(3)
                pe_marks_f.append(em.val("pe"))
                step_idx += 1
            em.wait("vector", "pe")
            for qh in range(NH):
                em.doi("vector", "dve",
                       lambda e, qh=qh: e.reciprocal(recip_sb[:, qh:qh + 1], acc_ap(qh, 128, 1)))
            em.wait("vector", "dve")
            for qh in range(NH):
                em.doi("vector", "dve",
                       lambda e, qh=qh: e.tensor_scalar(ln_tok[:, qh * 128:(qh + 1) * 128],
                                                        acc_ap(qh, 0, 128),
                                                        recip_sb[:, qh:qh + 1], None, OP.mult))
            em.wait("tensor", "dve")
            em.wait("tensor", "act")
            for bq in range(4):
                for k in range(4):
                    dch = 4 * bq + k
                    em.doi("tensor", "pe",
                           lambda e, dch=dch, k=k: e.transpose(pb[6][:, k * 128:(k + 1) * 128],
                                                               ln_tok[:, dch * 128:(dch + 1) * 128], ident[:, :]))
                em.wait("vector", "pe")
                em.doi("vector", "dve",
                       lambda e, bq=bq: e.tensor_copy(
                           bass.AP(x_lnT, 2048 + (4 * bq) * 128, [[8192, 128], [1, 512]]),
                           pb[6][:, :]))
                em.wait("tensor", "dve")
            bank_dve[0] = em.val("dve")
            for dch in range(NDC):
                for dg in range(4):
                    em.doi("tensor", "pe",
                           lambda e, dch=dch, dg=dg: e.matmul(pb[dg][:, :], attnT_ap(dch),
                                                              wo_ap(dch, dg * 512, 512),
                                                              start=(dch == 0), stop=(dch == NDC - 1)))
            em.wait("sync", "act")
            em.wait("sync", "qg")
            em.dma("qx", xstage[:, 0, :], src_attn.rearrange("(q p) d -> q p d", p=128)[i, :, :], "qx")
            em.wait("vector", "qx")
            em.wait("vector", "pe")
            for dg in range(4):
                em.doi("vector", "dve",
                       lambda e, dg=dg: e.tensor_tensor(xstage[:, 0, dg * 512:(dg + 1) * 512],
                                                        xstage[:, 0, dg * 512:(dg + 1) * 512],
                                                        pb[dg][:, :], OP.add))
            em.wait("gpsimd", "dve")
            em.dma("qg", dst.rearrange("(q p) d -> q p d", p=128)[i, :, :], xstage[:, 0, :], "qg")

    def program(em):
        em.dma("qx", ident[:, :], ident_in, "qx")
        em.dma("qx", consts_sb[:, :], consts_in, "qx")
        if stage != "x1":
            load_consts_attn(em)
        cc_base0 = em.val("cc")
        dequant_ag(em)
        ffn(em, x_in, y_out if stage == "x1" else xres_hbm, 0.5, cc_base=cc_base0)
        if stage in ("x2", "full"):
            proj_attn(em, xres_hbm.ap())
            cc_bm = ag_kv(em)
            wo_load(em)
            gating(em, cc_bm)
            flash(em, y_out if stage == "x2" else xres_hbm)
        if stage == "full":
            ffn(em, xres_hbm, y_out, 0.5)
        em.wait("gpsimd", "qg")

    def run_pass(role, eng):
        for s in sems.values():
            s[1] = 0
        em = Emit(role, eng, sems)
        program(em)

    @blk.sync
    def _(e):
        run_pass("sync", e)

    @blk.scalar
    def _(e):
        run_pass("scalar", e)

    @blk.vector
    def _(e):
        run_pass("vector", e)

    @blk.tensor
    def _(e):
        run_pass("tensor", e)

    @blk.gpsimd
    def _(e):
        run_pass("gpsimd", e)

    ctx.close()
    return nc


def make_inputs(full, c):
    rows = []
    for i in range(8):
        g = perm_block(c, i)
        rows.extend(range(g * 128, g * 128 + 128))
    rows = np.array(rows)
    gup = full["L0_ffn_gate_up_packed"]
    gus = full["L0_ffn_gate_up_scales"]
    dnp = full["L0_ffn_down_packed"]
    dns = full["L0_ffn_down_scales"]
    gp_pad = np.zeros((1536, 256), np.int32)
    gs_pad = np.zeros((1536, 16), np.float16)
    for k in range(6):
        g = 8 * k + c
        if g < NIC:
            gp_pad[k * 128:(k + 1) * 128] = gup[g * 128:(g + 1) * 128]
            gs_pad[k * 128:(k + 1) * 128] = gus[:, g * 128:(g + 1) * 128].T
            gp_pad[(6 + k) * 128:(7 + k) * 128] = gup[IH + g * 128:IH + (g + 1) * 128]
            gs_pad[(6 + k) * 128:(7 + k) * 128] = gus[:, IH + g * 128:IH + (g + 1) * 128].T
    d = {
        "x_own": np.ascontiguousarray(full["L0_attn_x"][0][rows]).astype(np.float32),
        "gu_packed": gp_pad,
        "gu_scales_t": gs_pad,
        "dn_packed": np.ascontiguousarray(dnp[256 * c:256 * c + 256]),
        "dn_scales_t": np.ascontiguousarray(dns[:, 256 * c:256 * c + 256].T),
        "ident_f32": np.eye(128, dtype=np.float32),
        "consts": np.tile(np.array([[EPS_LN, EPS_RMS, 1.0 / D, 0, 0, 0, 0, 0]], np.float32), (128, 1)),
        "wq": full["L0_attn_wq"].astype(np.float32),
        "wk": full["L0_attn_wk"].astype(np.float32),
        "wv": full["L0_attn_wv"].astype(np.float32),
        "wo": full["L0_attn_wo"].astype(np.float32),
    }
    pos = rows.astype(np.float64)
    invf = 1.0 / (10000.0 ** (np.arange(0, HD, 2, dtype=np.float64) / HD))
    ang = pos[None, :] * np.repeat(invf, 2)[:, None]
    C = np.cos(ang).astype(np.float32)
    Ssin = np.sin(ang).astype(np.float32)
    sgn = np.where(np.arange(HD) % 2 == 0, -1.0, 1.0).astype(np.float32)
    d["ropeC"] = np.ascontiguousarray(C)
    d["ropeS"] = np.ascontiguousarray(Ssin * sgn[:, None])
    P = np.zeros((128, 128), np.float32)
    for cc in range(128):
        P[cc, cc ^ 1] = 1.0
    d["pswap"] = P.astype(ml_dtypes.bfloat16)
    cn = np.where(np.arange(128)[None, :] > np.arange(128)[:, None], -10000.0, 0.0).astype(np.float32)
    d["causal_neg"] = np.ascontiguousarray(cn)
    colg = np.array([8 * (col % 8) + col // 8 for col in range(64)])
    colmask = np.zeros((8, 64), np.float32)
    ownsel = np.zeros((8, 64), np.float32)
    for i in range(8):
        own = perm_block(c, i)
        colmask[i] = (colg <= own).astype(np.float32)
        ownsel[i, 8 * c + i] = 1.0
    d["colmask"] = np.tile(colmask.reshape(1, 512), (128, 1))
    d["ownsel"] = np.tile(ownsel.reshape(1, 512), (128, 1))
    d["diag_en"] = np.tile(np.array([[0.0] + [1.0] * 7], np.float32), (128, 1))
    tA = np.zeros(296, np.float32)
    s = 0
    for i in range(8):
        own = perm_block(c, i)
        tA[s] = 0.0; s += 1
        for g in range(8 * i + 8):
            if i == 0:
                tA[s] = -10000.0 if g == own else 0.0
            else:
                tA[s] = 0.0 if g < own else -10000.0
            s += 1
    assert s == 296
    d["tA"] = np.tile(tA[None, :], (128, 1))
    return d, rows




_CACHED = {}


def kernel(L0_attn_x, L0_attn_wq, L0_attn_wk, L0_attn_wv, L0_attn_wo,
           L0_ffn_gate_up_packed, L0_ffn_gate_up_scales,
           L0_ffn_down_packed, L0_ffn_down_scales):
    full = {
        "L0_attn_x": np.asarray(L0_attn_x),
        "L0_attn_wq": np.asarray(L0_attn_wq),
        "L0_attn_wk": np.asarray(L0_attn_wk),
        "L0_attn_wv": np.asarray(L0_attn_wv),
        "L0_attn_wo": np.asarray(L0_attn_wo),
        "L0_ffn_gate_up_packed": np.asarray(L0_ffn_gate_up_packed),
        "L0_ffn_gate_up_scales": np.asarray(L0_ffn_gate_up_scales),
        "L0_ffn_down_packed": np.asarray(L0_ffn_down_packed),
        "L0_ffn_down_scales": np.asarray(L0_ffn_down_scales),
    }
    if "nc" not in _CACHED:
        _CACHED["nc"] = build("full")
    nc = _CACHED["nc"]
    in_maps, rowl = [], []
    for c in range(NCORE):
        d, rows = make_inputs(full, c)
        in_maps.append(d)
        rowl.append(rows)
    res = run_bass_kernel_spmd(nc, in_maps, core_ids=list(range(NCORE)))
    out = np.zeros((1, S, D), np.float32)
    for c in range(NCORE):
        out[0, rowl[c]] = res.results[c]["y_own"]
    return out
